# version 03 (final)
# baseline (speedup 1.0000x reference)
import math
from contextlib import ExitStack
import numpy as np
import ml_dtypes
import concourse.bass as bass
import concourse.mybir as mybir
from concourse.bass_utils import run_bass_kernel_spmd

F32 = mybir.dt.float32
BF16 = mybir.dt.bfloat16
AF = mybir.ActivationFunctionType
ALU = mybir.AluOpType
AX = mybir.AxisListType

N = 2048
DM = 1024
NT = N // 128
NCH = DM // 128
BF = ml_dtypes.bfloat16


class _Rec:
    __slots__ = ("name", "args", "kwargs", "incs")

    def __init__(self, name, args, kwargs):
        self.name, self.args, self.kwargs = name, args, kwargs
        self.incs = []

    def then_inc(self, sem, n):
        self.incs.append((sem, n))
        return self

    def run(self, eng):
        try:
            r = getattr(eng, self.name)(*self.args, **self.kwargs)
        except Exception as e:
            raise RuntimeError(f"failed op {self.name} args={[getattr(x, 'shape', x) for x in self.args]} kwargs={ {k: getattr(vv, 'shape', vv) for k, vv in self.kwargs.items()} } err={e}") from e
        for s, n in self.incs:
            r = r.then_inc(s, n)


class EngQ:
    SER_OPS = {"tensor_copy", "tensor_tensor", "tensor_scalar",
               "tensor_add", "tensor_sub", "tensor_mul", "tensor_max", "reduce_sum",
               "reduce_max", "reciprocal", "activation", "mul", "add", "copy",
               "scalar_tensor_tensor", "memset", "max", "tensor_single_scalar"}

    def __init__(self):
        self.recs = []
        self.ser = None

    def __getattr__(self, name):
        def call(*args, **kwargs):
            if self.ser is not None and name in self.SER_OPS:
                sem, q = self.ser
                w = _Rec("wait_ge", (sem.h, sem.n), {})
                self.recs.append(w)
                r = _Rec(name, args, kwargs)
                self.recs.append(r)
                sem.inc(r)
                return r
            r = _Rec(name, args, kwargs)
            self.recs.append(r)
            return r
        return call


class Sem:
    def __init__(self, h):
        self.h = h
        self.n = 0

    def inc(self, rec, k=1):
        rec.then_inc(self.h, k)
        self.n += k
        return rec

    def mark(self):
        return (self, self.n)


SERIALIZE = False


class Ctx:
    ENGINES = ("tensor", "vector", "scalar", "gpsimd", "sync")

    def __init__(self, nc, stack):
        self.nc = nc
        self.stack = stack
        for e in self.ENGINES:
            setattr(self, e, EngQ())
        self._sems = {}
        if SERIALIZE:
            self._ser = Sem(stack.enter_context(nc.semaphore("ser")))
            for e in ("tensor", "vector", "scalar", "gpsimd"):
                getattr(self, e).ser = (self._ser, getattr(self, e))

    def sem(self, name):
        if name not in self._sems:
            h = self.stack.enter_context(self.nc.semaphore(name))
            self._sems[name] = Sem(h)
        return self._sems[name]

    def wait(self, q, mark):
        sem, val = mark
        if val > 0:
            q.wait_ge(sem.h, val)

    def install(self):
        with self.nc.Block() as block:
            for e in self.ENGINES:
                q = getattr(self, e)
                if q.recs:
                    getattr(block, e)(lambda eng, q=q: [r.run(eng) for r in q.recs])


class PsumPool:

    def __init__(self, ctx, ap, nb):
        self.ctx = ctx
        self.ap = ap
        self.nb = nb
        self.busy = [None] * nb
        self.rr = 0

    def acquire(self, q):
        i = self.rr % self.nb
        self.rr += 1
        if self.busy[i] is not None:
            self.ctx.wait(q, self.busy[i])
            self.busy[i] = None
        return i

    def release(self, i, mark):
        self.busy[i] = mark


class B:

    def __init__(self, ctx):
        self.ctx = ctx
        self.nc = ctx.nc
        self.s_t = ctx.sem("s_t")
        self.s_v = ctx.sem("s_v")
        self.s_a = ctx.sem("s_a")
        self.s_g = ctx.sem("s_g")
        self.s_ld = ctx.sem("s_ld")
        self.s_ldx = ctx.sem("s_ldx")
        self.s_st = ctx.sem("s_st")
        self.consts = {}

    def sbuf(self, name, shape, dtype):
        return self.ctx.stack.enter_context(self.nc.sbuf_tensor(name, shape, dtype))

    def psum_cm(self, name, shape, dtype):
        cm = self.nc.psum_tensor(name, shape, dtype)
        return cm, cm.__enter__()


def evac_q(b, engine):
    if engine == "vector":
        return b.ctx.vector, b.s_v
    elif engine == "scalar":
        return b.ctx.scalar, b.s_a
    else:
        return b.ctx.gpsimd, b.s_g


def make_identity_bf16():
    return np.eye(128, dtype=np.float32).astype(BF)


def chunk_weight(w):
    K, O = w.shape
    nk = (K + 127) // 128
    out = np.zeros((nk, 128, O), dtype=np.float32)
    out.reshape(nk * 128, O)[:K] = np.asarray(w, dtype=np.float32)
    return np.ascontiguousarray(out)


def tm_view(hbm_2d_ap):
    return hbm_2d_ap.rearrange("(t p) d -> p t d", p=128)


def emit_ln(b, x_tm, yscr, stats, sqscr, deps=(), width=DM, nt=NT, eps=1e-5,
            out_fp32=None):
    ctx, v, a, g = b.ctx, b.ctx.vector, b.ctx.scalar, b.ctx.gpsimd
    for m in deps:
        ctx.wait(g, m)
        ctx.wait(a, m)
        ctx.wait(v, m)
    sums = stats[:, 0:nt]
    sumsq = stats[:, nt:2 * nt]
    mu = stats[:, 2 * nt:3 * nt]
    rstd = stats[:, 3 * nt:4 * nt]
    A = stats[:, 4 * nt:5 * nt]
    musq = stats[:, 5 * nt:6 * nt]
    ve = stats[:, 6 * nt:7 * nt]
    inv = stats[:, 7 * nt:8 * nt]
    for t in range(nt):
        sl = x_tm[:, t * width:(t + 1) * width]
        v.reduce_sum(sums[:, t:t + 1], sl, axis=AX.X)
        a.activation(sqscr[:], sl, AF.Square,
                     accum_out=sumsq[:, t:t + 1])
    b.s_v.inc(v.recs[-1])
    vmark0 = b.s_v.mark()
    b.s_a.inc(a.recs[-1])
    amark = b.s_a.mark()
    ctx.wait(a, vmark0)
    b.s_a.inc(a.mul(mu, sums, 1.0 / width))
    amark2 = b.s_a.mark()
    ctx.wait(v, amark)
    v.tensor_scalar(A, sumsq, 1.0 / width, eps, ALU.mult, ALU.add)
    ctx.wait(v, amark2)
    b.s_v.inc(v.tensor_tensor(musq, mu, mu, ALU.mult))
    vmark = b.s_v.mark()
    ctx.wait(g, vmark)
    b.s_g.inc(g.tensor_tensor(ve, A, musq, ALU.subtract))
    gmark2 = b.s_g.mark()
    ctx.wait(v, gmark2)
    b.s_v.inc(v.reciprocal(inv, ve))
    vmark2 = b.s_v.mark()
    ctx.wait(a, vmark2)
    b.s_a.inc(a.activation(rstd, inv, AF.Sqrt, bias=0.0, scale=1.0))
    stats_mark = b.s_a.mark()

    state = {"first": True}

    def apply_tile(t, dst):
        if state["first"]:
            ctx.wait(g, stats_mark)
            state["first"] = False
        r = g.tensor_scalar(dst, x_tm[:, t * width:(t + 1) * width],
                            mu[:, t:t + 1], rstd[:, t:t + 1],
                            ALU.subtract, ALU.mult)
        b.s_g.inc(r)
        return b.s_g.mark()

    return apply_tile


def emit_tm_to_fm_tiles(b, get_src_mark, yT, ident, psT_pool, nch=NCH, nt=NT,
                        shift_chunks=(), evac="vector", fold=None, n_cols=None):
    ctx = b.ctx
    t_q = ctx.tensor
    ev, ev_sem = evac_q(b, evac)
    if n_cols is None:
        n_cols = nt * 128
    for t in range(nt):
        src, mark = get_src_mark(t)
        ctx.wait(t_q, mark)
        bank = psT_pool.acquire(t_q)
        for c in range(nch):
            mm = t_q.transpose(psT_pool.ap[:, bank, c * 128:(c + 1) * 128],
                               src[:, c * 128:(c + 1) * 128], ident[:])
        b.s_t.inc(mm)
        ctx.wait(ev, b.s_t.mark())
        for c in range(nch):
            base = c * n_cols
            if c in shift_chunks:
                lo = base + t * 128 + 1
                w = min(128, base + n_cols - lo)
            else:
                lo = base + t * 128
                w = 128
            srcap = psT_pool.ap[:, bank, c * 128:c * 128 + w]
            dst = yT[:, lo:lo + w]
            if fold is not None:
                g_col, b_col = fold(c)
                e = ev.scalar_tensor_tensor(dst, srcap, g_col,
                                            b_col.broadcast_to((128, w)),
                                            ALU.mult, ALU.add)
            else:
                e = ev.tensor_copy(dst, srcap)
        ev_sem.inc(e)
        psT_pool.release(bank, ev_sem.mark())
    return ev_sem.mark()


def emit_ln_to_fm(b, x_tm, yT, ident, psT_pool, yscr, stats, sqscr, deps=(),
                  shift_chunks=(), evac="vector", fold=None):
    apply_tile = emit_ln(b, x_tm, yscr, stats, sqscr, deps=deps)
    parity_busy = [None, None]
    ctx = b.ctx

    def get_src(t):
        p = t % 2
        if parity_busy[p] is not None:
            ctx.wait(ctx.gpsimd, parity_busy[p])
        m = apply_tile(t, yscr[:, p, :])
        return yscr[:, p, :], m

    t_q = ctx.tensor
    ev, ev_sem = evac_q(b, evac)
    for t in range(NT):
        src, mark = get_src(t)
        ctx.wait(t_q, mark)
        bank = psT_pool.acquire(t_q)
        for c in range(NCH):
            mm = t_q.transpose(psT_pool.ap[:, bank, c * 128:(c + 1) * 128],
                               src[:, c * 128:(c + 1) * 128], ident[:])
        b.s_t.inc(mm)
        parity_busy[t % 2] = b.s_t.mark()
        ctx.wait(ev, b.s_t.mark())
        for c in range(NCH):
            base = c * N
            if c in shift_chunks:
                lo = base + t * 128 + 1
                w = min(128, base + N - lo)
            else:
                lo = base + t * 128
                w = 128
            srcap = psT_pool.ap[:, bank, c * 128:c * 128 + w]
            dst = yT[:, lo:lo + w]
            if fold is not None:
                g_col, b_col = fold(c)
                e = ev.scalar_tensor_tensor(dst, srcap, g_col,
                                            b_col.broadcast_to((128, w)),
                                            ALU.mult, ALU.add)
            else:
                e = ev.tensor_copy(dst, srcap)
        ev_sem.inc(e)
        psT_pool.release(bank, ev_sem.mark())
    return ev_sem.mark()


def emit_proj_fm(b, dst_fm, yT, wget, nkc, oc_list, pool, ready=(),
                 bias_col=None, evac="vector", tgw=512, n_cols=N,
                 dst_stride=None, dst_off=0, fold=None, out_rows=128):
    ctx = b.ctx
    t_q = ctx.tensor
    ev, ev_sem = evac_q(b, evac)
    for m in ready:
        ctx.wait(t_q, m)
    if dst_stride is None:
        dst_stride = n_cols
    ntg = n_cols // tgw
    for j, oc in enumerate(oc_list):
        for g in range(ntg):
            bank = pool.acquire(t_q)
            ps = pool.ap[0:out_rows, bank, 0:tgw]
            for c in range(nkc):
                mm = t_q.matmul(ps, wget(c)[:, oc * 128:oc * 128 + out_rows],
                                yT[:, c * n_cols + g * tgw:c * n_cols + (g + 1) * tgw],
                                start=(c == 0), stop=(c == nkc - 1))
            b.s_t.inc(mm)
            ctx.wait(ev, b.s_t.mark())
            dst = dst_fm[0:out_rows, dst_off + j * dst_stride + g * tgw:
                         dst_off + j * dst_stride + (g + 1) * tgw]
            if fold is not None:
                e = fold(ev, dst, ps, oc, g)
            elif bias_col is not None:
                e = ev.tensor_scalar(dst, ps, bias_col(oc), None, ALU.add)
            else:
                e = ev.tensor_copy(dst, ps)
            ev_sem.inc(e)
            pool.release(bank, ev_sem.mark())
    return ev_sem.mark()


def emit_proj_tm(b, dst_tm, dst_stride, yT, wget, nkc, O, pool, ones_row,
                 ready=(), bias_row=None, evac="vector", add_to=None,
                 n_cols=N, fold=None, nt=NT, dst_off=0, wget_cols=None):
    ctx = b.ctx
    t_q = ctx.tensor
    ev, ev_sem = evac_q(b, evac)
    for m in ready:
        ctx.wait(t_q, m)
    ngrp = (O + 511) // 512
    for t in range(nt):
        for g in range(ngrp):
            o0 = g * 512
            ow = min(512, O - o0)
            bank = pool.acquire(t_q)
            ps = pool.ap[:, bank, 0:ow]
            for c in range(nkc):
                w_ap = wget(c)
                mm = t_q.matmul(ps, yT[:, c * n_cols + t * 128:c * n_cols + (t + 1) * 128],
                                w_ap[:, o0:o0 + ow],
                                start=(c == 0), stop=(c == nkc - 1 and bias_row is None))
            if bias_row is not None:
                mm = t_q.matmul(ps, ones_row[:, 0:128], bias_row[:, o0:o0 + ow],
                                start=False, stop=True)
            b.s_t.inc(mm)
            ctx.wait(ev, b.s_t.mark())
            dst = dst_tm[:, dst_off + t * dst_stride + o0:dst_off + t * dst_stride + o0 + ow]
            if fold is not None:
                e = fold(ev, dst, ps, t, o0, ow)
            elif add_to is not None:
                e = ev.tensor_add(dst, ps,
                                  add_to[:, dst_off + t * dst_stride + o0:
                                         dst_off + t * dst_stride + o0 + ow])
            else:
                e = ev.tensor_copy(dst, ps)
            ev_sem.inc(e)
            pool.release(bank, ev_sem.mark())
    return ev_sem.mark()


def build_l1_attn_b(nc, stack, dbg=None):
    ctx = Ctx(nc, stack)
    b = B(ctx)
    d = {}
    d["x_in"] = nc.dram_tensor("x_in", [N, DM], F32, kind="ExternalInput").ap()
    d["wq"] = nc.dram_tensor("wq", [8, 128, DM], F32, kind="ExternalInput").ap()
    d["wk"] = nc.dram_tensor("wk", [8, 128, DM], F32, kind="ExternalInput").ap()
    d["wv"] = nc.dram_tensor("wv", [8, 128, DM], F32, kind="ExternalInput").ap()
    d["wo"] = nc.dram_tensor("wo", [8, 128, DM], F32, kind="ExternalInput").ap()
    d["bq_c"] = nc.dram_tensor("bq_c", [128, 8], F32, kind="ExternalInput").ap()
    d["bk_c"] = nc.dram_tensor("bk_c", [128, 8], F32, kind="ExternalInput").ap()
    d["bv_r"] = nc.dram_tensor("bv_r", [1, DM], F32, kind="ExternalInput").ap()
    d["bo_r"] = nc.dram_tensor("bo_r", [1, DM], F32, kind="ExternalInput").ap()
    d["ident"] = nc.dram_tensor("ident", [128, 128], BF16, kind="ExternalInput").ap()
    d["x_out"] = nc.dram_tensor("x_out", [N, DM], F32, kind="ExternalOutput").ap()

    H, DH = 8, 128

    x_tm = b.sbuf("x_tm", [128, NT * DM], F32)
    yT = b.sbuf("yT", [128, NCH * N], BF16)
    yscr = b.sbuf("yscr", [128, 2, DM], BF16)
    stats = b.sbuf("stats", [128, 8 * NT], F32)
    sqscr = b.sbuf("sqscr", [128, DM], BF16)
    ident = b.sbuf("identt", [128, 128], BF16)
    ones_row = b.sbuf("ones_row", [1, 128], BF16)
    attnout = b.sbuf("attnout", [128, H * N], BF16)
    qh = b.sbuf("qh", [128, N], BF16)
    kh = b.sbuf("kh", [128, N], BF16)
    vh = b.sbuf("vh", [128, N], BF16)
    sT = b.sbuf("sT", [128, 2, N], BF16)
    whq = b.sbuf("whq", [128, 8 * 128], BF16)
    whk = b.sbuf("whk", [128, 8 * 128], BF16)
    whv = b.sbuf("whv", [128, 8 * 128], BF16)
    wob = b.sbuf("wob", [128, 8 * DM], BF16)
    bqc = b.sbuf("bqc", [128, 8], F32)
    bkc = b.sbuf("bkc", [128, 8], F32)
    bvr = b.sbuf("bvr", [1, DM], BF16)
    bor = b.sbuf("bor", [1, DM], BF16)

    sy, g = ctx.sync, ctx.gpsimd
    b.s_ldx.inc(sy.dma_start(out=x_tm[:].rearrange("p (t d) -> p t d", t=NT),
                             in_=tm_view(d["x_in"])), 16)
    b.s_ldx.inc(sy.dma_start(out=bqc[:], in_=d["bq_c"]), 16)
    b.s_ldx.inc(sy.dma_start(out=bkc[:], in_=d["bk_c"]), 16)
    b.s_ld.inc(g.dma_start(out=ident[:], in_=d["ident"]), 16)
    b.s_ld.inc(g.dma_start(out=bvr[:], in_=d["bv_r"]), 16)
    b.s_ld.inc(g.dma_start(out=bor[:], in_=d["bo_r"]), 16)
    for c in range(8):
        b.s_ld.inc(g.dma_start(out=wob[:, c * DM:(c + 1) * DM], in_=d["wo"][c]), 16)
    x_mark = b.s_ldx.mark()
    const_mark = b.s_ld.mark()
    ctx.vector.wait_ge(b.s_ldx.h, x_mark[1])
    ctx.scalar.wait_ge(b.s_ldx.h, x_mark[1])
    ctx.tensor.wait_ge(b.s_ld.h, const_mark[1])

    b.s_v.inc(ctx.vector.memset(ones_row[:], 1.0))

    cm4, ps4 = b.psum_cm("ps4", [128, 4, 512], F32)
    pool4 = PsumPool(ctx, ps4, 4)
    cmT, psT = b.psum_cm("psT", [128, 2, 1024], BF16)
    psT_pool = PsumPool(ctx, psT, 2)

    ln_mark = emit_ln_to_fm(b, x_tm, yT, ident, psT_pool, yscr, stats, sqscr,
                            deps=[x_mark])
    cmT.__exit__(None, None, None)
    cm2, psAV = b.psum_cm("psAV", [128, 4, 512], F32)

    def load_head_w(h, after_mark):
        if after_mark is not None:
            ctx.wait(g, after_mark)
        for nm, buf, dd in (("q", whq, d["wq"]), ("k", whk, d["wk"]), ("v", whv, d["wv"])):
            b.s_ld.inc(g.dma_start(out=buf[:].rearrange("p (c o) -> p c o", c=8),
                                   in_=dd[:, :, h * 128:(h + 1) * 128]
                                   .rearrange("c p o -> p c o")), 16)
        return b.s_ld.mark()

    wmark = load_head_w(0, None)
    for h in range(H):
        t_q, v = ctx.tensor, ctx.vector
        qm = emit_proj_fm(b, qh, yT, lambda c: whq[:, c * 128:(c + 1) * 128],
                          NCH, [0], pool4, ready=[ln_mark, wmark],
                          bias_col=lambda oc: bqc[:, h:h + 1], dst_stride=N)
        km = emit_proj_fm(b, kh, yT, lambda c: whk[:, c * 128:(c + 1) * 128],
                          NCH, [0], pool4, ready=[],
                          bias_col=lambda oc: bkc[:, h:h + 1], dst_stride=N)
        vm = emit_proj_tm(b, vh, 128, yT, lambda c: whv[:, c * 128:(c + 1) * 128],
                          NCH, 128, pool4, ones_row, ready=[],
                          bias_row=bvr[:, h * 128:(h + 1) * 128])
        if h + 1 < H:
            wnext_gate = b.s_t.mark()
            wmark = load_head_w(h + 1, wnext_gate)
        evmarks = {}
        av_t_marks = {}
        for kt in range(NT + 1):
            if kt < NT:
                if kt >= 2:
                    par_gate = av_t_marks[kt - 2]
                else:
                    par_gate = None
                for gq in range(4):
                    bank = pool4.acquire(t_q)
                    mm = t_q.matmul(pool4.ap[:, bank, :],
                                    kh[:, kt * 128:(kt + 1) * 128],
                                    qh[:, gq * 512:(gq + 1) * 512],
                                    start=True, stop=True)
                    b.s_t.inc(mm)
                    ctx.wait(v, b.s_t.mark())
                    if par_gate is not None:
                        ctx.wait(v, par_gate)
                        par_gate = None
                    e = v.tensor_copy(sT[:, kt % 2, gq * 512:(gq + 1) * 512],
                                      pool4.ap[:, bank, :])
                    b.s_v.inc(e)
                    pool4.release(bank, b.s_v.mark())
                evmarks[kt] = b.s_v.mark()
            if kt >= 1:
                k0 = kt - 1
                ctx.wait(t_q, evmarks[k0])
                for gq in range(4):
                    mm = t_q.matmul(psAV[:, gq, :], vh[:, k0 * 128:(k0 + 1) * 128],
                                    sT[:, k0 % 2, gq * 512:(gq + 1) * 512],
                                    start=(k0 == 0), stop=(k0 == NT - 1))
                    b.s_t.inc(mm)
                av_t_marks[k0] = b.s_t.mark()
        ctx.wait(v, b.s_t.mark())
        for gq in range(4):
            e = v.tensor_copy(attnout[:, h * N + gq * 512:h * N + (gq + 1) * 512],
                              psAV[:, gq, :])
            b.s_v.inc(e)
        av_done = b.s_v.mark()
        ctx.wait(t_q, av_done)

    fin = emit_proj_tm(b, x_tm, DM, attnout,
                       lambda c: wob[:, c * DM:(c + 1) * DM],
                       8, DM, pool4, ones_row, ready=[b.s_v.mark()],
                       bias_row=bor, add_to=x_tm)
    ctx.wait(ctx.sync, fin)
    b.s_st.inc(ctx.sync.dma_start(out=tm_view(d["x_out"]),
                                  in_=x_tm[:].rearrange("p (t d) -> p t d", t=NT)), 16)
    if dbg:
        dbg_bufs = {"yT": yT, "attnout": attnout, "qh": qh, "kh": kh, "vh": vh,
                    "sT": sT[:, 0, :], "stats": stats, "x_tm": x_tm, "whq": whq}
        for nm in dbg:
            buf = dbg_bufs[nm]
            if not isinstance(buf, bass.AP):
                buf = buf[:]
            shp = list(buf.shape)
            dt_ = F32 if nm in ("stats", "x_tm") else BF16
            dd = nc.dram_tensor("dbg_" + nm, shp, dt_, kind="ExternalOutput").ap()
            b.s_st.inc(ctx.sync.dma_start(out=dd, in_=buf), 16)
    ctx.sync.wait_ge(b.s_st.h, b.s_st.n)
    cm2.__exit__(None, None, None)
    cm4.__exit__(None, None, None)
    ctx.install()
    return d


def prep_l1_attn_b(ins):
    w_qkv = np.asarray(ins["L1_attn_b_w_qkv"])
    b_qkv = np.asarray(ins["L1_attn_b_b_qkv"])
    w_o = np.asarray(ins["L1_attn_b_w_o"])
    b_o = np.asarray(ins["L1_attn_b_b_o"])
    H, DH = 8, 128
    idx = np.arange(1024)
    h = idx // 128
    dd = idx % 128
    perm = dd * 8 + h
    out = {}
    for si, nm in ((0, "wq"), (1, "wk"), (2, "wv")):
        Wt = w_qkv[si * 1024 + perm, :].T
        out[nm] = chunk_weight(Wt)
        bias = b_qkv[si * 1024 + perm]
        if nm == "wv":
            out["bv_r"] = np.ascontiguousarray(bias[None, :].astype(np.float32))
        else:
            out[("bq_c" if nm == "wq" else "bk_c")] = \
                np.ascontiguousarray(bias.reshape(H, DH).T.astype(np.float32))
    out["wo"] = chunk_weight(w_o.T)
    out["bo_r"] = np.ascontiguousarray(b_o[None, :].astype(np.float32))
    out["ident"] = make_identity_bf16()
    return out


def run_l1_attn_b(x_full, ins, dbg=None):
    prep = prep_l1_attn_b(ins)
    with ExitStack() as stack:
        nc = bass.Bass("TRN2", target_bir_lowering=False)
        build_l1_attn_b(nc, stack, dbg=dbg)
        in_maps = []
        for i in range(8):
            m = {k: v for k, v in prep.items()}
            m["x_in"] = np.ascontiguousarray(x_full[i])
            in_maps.append(m)
        res = run_bass_kernel_spmd(nc, in_maps, core_ids=list(range(8)))
    out = np.stack([res.results[i]["x_out"] for i in range(8)], axis=0)
    if dbg:
        return out, res.results
    return out


def build_l1_attn_a(nc, stack, dbg=None):
    ctx = Ctx(nc, stack)
    b = B(ctx)
    d = {}
    d["x_in"] = nc.dram_tensor("x_in", [N, DM], F32, kind="ExternalInput").ap()
    d["wq"] = nc.dram_tensor("wq", [8, 128, DM], F32, kind="ExternalInput").ap()
    d["wk"] = nc.dram_tensor("wk", [8, 128, DM], F32, kind="ExternalInput").ap()
    d["wv"] = nc.dram_tensor("wv", [8, 128, DM], F32, kind="ExternalInput").ap()
    d["wo"] = nc.dram_tensor("wo", [8, 128, DM], F32, kind="ExternalInput").ap()
    d["bo_r"] = nc.dram_tensor("bo_r", [1, DM], F32, kind="ExternalInput").ap()
    d["ident"] = nc.dram_tensor("ident", [128, 128], BF16, kind="ExternalInput").ap()
    d["cmask"] = nc.dram_tensor("cmask", [128, 128], BF16, kind="ExternalInput").ap()
    d["x_out"] = nc.dram_tensor("x_out", [N, DM], F32, kind="ExternalOutput").ap()

    x_tm = b.sbuf("x_tm", [128, NT * DM], F32)
    yT = b.sbuf("yT", [128, NCH * N], BF16)
    yscr = b.sbuf("yscr", [128, 2, DM], BF16)
    stats = b.sbuf("stats", [128, 8 * NT], F32)
    sqscr = b.sbuf("sqscr", [128, DM], BF16)
    ident = b.sbuf("identt", [128, 128], BF16)
    ones_row = b.sbuf("ones_row", [1, 128], BF16)
    cmask = b.sbuf("cmask_t", [128, 128], BF16)
    attnout = b.sbuf("attnout", [128, NCH * N], BF16)
    qhp = b.sbuf("qhp", [128, N], BF16)
    khp = b.sbuf("khp", [128, N], BF16)
    vhp = b.sbuf("vhp", [128, NT * 128], BF16)
    s0 = b.sbuf("s0", [128, N], F32)
    s1 = b.sbuf("s1", [128, N], F32)
    e_b = b.sbuf("e_b", [128, N], BF16)
    en = b.sbuf("en", [128, N], BF16)
    eT = b.sbuf("eT", [128, 2, 128], BF16)
    rowsum = b.sbuf("rowsum", [128, 4], F32)
    whq = b.sbuf("whq", [128, 8 * 128], BF16)
    whk = b.sbuf("whk", [128, 8 * 128], BF16)
    whv = b.sbuf("whv", [128, 8 * 128], BF16)
    wob = b.sbuf("wob", [128, 8 * DM], BF16)
    bor = b.sbuf("bor", [1, DM], BF16)

    sy, g = ctx.sync, ctx.gpsimd
    b.s_ldx.inc(sy.dma_start(out=x_tm[:].rearrange("p (t d) -> p t d", t=NT),
                             in_=tm_view(d["x_in"])), 16)
    b.s_ldx.inc(sy.dma_start(out=cmask[:], in_=d["cmask"]), 16)
    b.s_ld.inc(g.dma_start(out=ident[:], in_=d["ident"]), 16)
    b.s_ld.inc(g.dma_start(out=bor[:], in_=d["bo_r"]), 16)
    x_mark = b.s_ldx.mark()
    const_mark = b.s_ld.mark()
    ctx.tensor.wait_ge(b.s_ld.h, const_mark[1])
    ctx.tensor.wait_ge(b.s_ldx.h, x_mark[1])
    b.s_v.inc(ctx.vector.memset(ones_row[:], 1.0))

    cm4, ps4 = b.psum_cm("ps4", [128, 4, 512], F32)
    pool4 = PsumPool(ctx, ps4, 4)
    cmT, psT = b.psum_cm("psT", [128, 2, 1024], BF16)
    psT_pool = PsumPool(ctx, psT, 2)
    cmAV, psAV = b.psum_cm("psAV", [128, 512], F32)

    ln_mark = emit_ln_to_fm(b, x_tm, yT, ident, psT_pool, yscr, stats, sqscr,
                            deps=[x_mark])

    def load_hp_w(hp, after_mark):
        if after_mark is not None:
            ctx.wait(g, after_mark)
        for buf, dd in ((whq, d["wq"]), (whk, d["wk"]), (whv, d["wv"])):
            b.s_ld.inc(g.dma_start(out=buf[:].rearrange("p (c o) -> p c o", c=8),
                                   in_=dd[:, :, hp * 128:(hp + 1) * 128]
                                   .rearrange("c p o -> p c o")), 16)
        return b.s_ld.mark()

    t_q, v, a = ctx.tensor, ctx.vector, ctx.scalar
    wmark = load_hp_w(0, None)
    prev_sub_mark = None
    for hp in range(8):
        emit_proj_fm(b, qhp, yT, lambda c: whq[:, c * 128:(c + 1) * 128],
                     NCH, [0], pool4, ready=[ln_mark, wmark], dst_stride=N)
        emit_proj_fm(b, khp, yT, lambda c: whk[:, c * 128:(c + 1) * 128],
                     NCH, [0], pool4, dst_stride=N)
        vm = emit_proj_tm(b, vhp, 128, yT, lambda c: whv[:, c * 128:(c + 1) * 128],
                          NCH, 128, pool4, ones_row)
        if hp + 1 < 8:
            wmark = load_hp_w(hp + 1, b.s_t.mark())
        for hi in range(2):
            h0 = 64 * hi
            av_marks = {}
            for qt in range(NT):
                kmax = (qt + 1) * 128
                ngr = (kmax + 511) // 512
                banks = []
                for gr in range(ngr):
                    k0 = gr * 512
                    kw = min(512, kmax - k0)
                    bank = pool4.acquire(t_q)
                    last = (gr == ngr - 1)
                    mm = t_q.matmul(pool4.ap[:, bank, 0:kw],
                                    qhp[h0:h0 + 64, qt * 128:(qt + 1) * 128],
                                    khp[h0:h0 + 64, k0:k0 + kw],
                                    start=True, stop=not last)
                    if last:
                        mm = t_q.matmul(pool4.ap[:, bank, kw - 128:kw],
                                        ident[:], cmask[:], start=False, stop=True)
                    b.s_t.inc(mm)
                    banks.append((bank, k0, kw, b.s_t.mark()))
                if prev_sub_mark is not None:
                    ctx.wait(a, prev_sub_mark)
                for bank, k0, kw, mk in banks:
                    ctx.wait(a, mk)
                    e = a.activation(s0[:, k0:k0 + kw], pool4.ap[:, bank, 0:kw],
                                     AF.Copy, bias=0.0, scale=0.125)
                    b.s_a.inc(e)
                    pool4.release(bank, b.s_a.mark())
                ctx.wait(v, b.s_a.mark())
                b.s_v.inc(v.reduce_max(rowsum[:, 2:3], s0[:, 0:kmax], axis=AX.X))
                mmax = b.s_v.mark()
                ctx.wait(g, mmax)
                gg = g.tensor_scalar(s1[:, 0:kmax], s0[:, 0:kmax],
                                     rowsum[:, 2:3], None, ALU.subtract)
                b.s_g.inc(gg)
                prev_sub_mark = b.s_g.mark()
                ctx.wait(a, b.s_g.mark())
                b.s_a.inc(a.activation(e_b[:, 0:kmax], s1[:, 0:kmax], AF.Exp,
                                       accum_out=rowsum[:, 0:1]))
                ctx.wait(v, b.s_a.mark())
                b.s_v.inc(v.reciprocal(rowsum[:, 1:2], rowsum[:, 0:1]))
                ctx.wait(g, b.s_v.mark())
                b.s_g.inc(g.tensor_scalar(en[:, 0:kmax], e_b[:, 0:kmax],
                                          rowsum[:, 1:2], None, ALU.mult))
                en_mark = b.s_g.mark()
                ctx.wait(t_q, en_mark)
                for kt in range(qt + 1):
                    bankT = psT_pool.acquire(t_q)
                    tr = t_q.transpose(psT[:, bankT, 0:128],
                                       en[:, kt * 128:(kt + 1) * 128], ident[:])
                    b.s_t.inc(tr)
                    trm = b.s_t.mark()
                    ctx.wait(v, trm)
                    if kt >= 2:
                        ctx.wait(v, av_marks[kt - 2])
                    b.s_v.inc(v.tensor_copy(eT[:, kt % 2, :], psT[:, bankT, 0:128]))
                    evm = b.s_v.mark()
                    psT_pool.release(bankT, evm)
                    ctx.wait(t_q, evm)
                    mm = t_q.matmul(psAV[0:64, 0:128],
                                    vhp[:, kt * 128 + hi * 64:kt * 128 + hi * 64 + 64],
                                    eT[:, kt % 2, :],
                                    start=(kt == 0), stop=(kt == qt))
                    b.s_t.inc(mm)
                    av_marks[kt] = b.s_t.mark()
                ctx.wait(v, b.s_t.mark())
                dst = attnout[hi * 64:hi * 64 + 64, hp * N + qt * 128:hp * N + (qt + 1) * 128]
                b.s_v.inc(v.tensor_copy(dst, psAV[0:64, 0:128]))
                avem = b.s_v.mark()
                ctx.wait(t_q, avem)

    for c in range(8):
        b.s_ld.inc(g.dma_start(out=wob[:, c * DM:(c + 1) * DM], in_=d["wo"][c]), 16)
    ctx.wait(t_q, b.s_ld.mark())
    fin = emit_proj_tm(b, x_tm, DM, attnout,
                       lambda c: wob[:, c * DM:(c + 1) * DM],
                       8, DM, pool4, ones_row, ready=[b.s_v.mark()],
                       bias_row=bor, add_to=x_tm)
    ctx.wait(ctx.sync, fin)
    b.s_st.inc(ctx.sync.dma_start(out=tm_view(d["x_out"]),
                                  in_=x_tm[:].rearrange("p (t d) -> p t d", t=NT)), 16)
    if dbg:
        dbg_bufs = {"yT": yT, "attnout": attnout, "qhp": qhp, "khp": khp,
                    "vhp": vhp, "s0": s0, "e_b": e_b, "en": en, "stats": stats}
        for nm in dbg:
            buf = dbg_bufs[nm]
            if not isinstance(buf, bass.AP):
                buf = buf[:]
            dt_ = F32 if nm in ("stats", "s0") else BF16
            dd_ = nc.dram_tensor("dbg_" + nm, list(buf.shape), dt_, kind="ExternalOutput").ap()
            b.s_st.inc(ctx.sync.dma_start(out=dd_, in_=buf), 16)
    ctx.sync.wait_ge(b.s_st.h, b.s_st.n)
    cmAV.__exit__(None, None, None)
    cmT.__exit__(None, None, None)
    cm4.__exit__(None, None, None)
    ctx.install()
    return d


def prep_l1_attn_a(ins):
    WQ = np.asarray(ins["L1_attn_a_W_Q"]).reshape(1024, 1024)
    WK = np.asarray(ins["L1_attn_a_W_K"]).reshape(1024, 1024)
    WV = np.asarray(ins["L1_attn_a_W_V"]).reshape(1024, 1024)
    WO = np.transpose(np.asarray(ins["L1_attn_a_W_out"]), (1, 0, 2)).reshape(1024, 1024)
    bo = np.asarray(ins["L1_attn_a_b_out"])
    cmask = np.where(np.arange(128)[:, None] >= np.arange(128)[None, :],
                     np.float32(0.0), np.float32(-1e30)).astype(BF)
    return {"wq": chunk_weight(WQ), "wk": chunk_weight(WK), "wv": chunk_weight(WV),
            "wo": chunk_weight(WO), "bo_r": np.ascontiguousarray(bo[None, :].astype(np.float32)),
            "ident": make_identity_bf16(), "cmask": cmask}


def run_l1_attn_a(x_full, ins, dbg=None):
    prep = prep_l1_attn_a(ins)
    with ExitStack() as stack:
        nc = bass.Bass("TRN2", target_bir_lowering=False)
        build_l1_attn_a(nc, stack, dbg=dbg)
        in_maps = []
        for i in range(8):
            m = dict(prep)
            m["x_in"] = np.ascontiguousarray(x_full[i])
            in_maps.append(m)
        res = run_bass_kernel_spmd(nc, in_maps, core_ids=list(range(8)))
    out = np.stack([res.results[i]["x_out"] for i in range(8)], axis=0)
    if dbg:
        return out, res.results
    return out


def build_l1_ffn(nc, stack, dbg=None):
    ctx = Ctx(nc, stack)
    b = B(ctx)
    d = {}
    d["x_in"] = nc.dram_tensor("x_in", [N, DM], F32, kind="ExternalInput").ap()
    d["rw"] = nc.dram_tensor("rw", [8, 128, 16], F32, kind="ExternalInput").ap()
    d["rb"] = nc.dram_tensor("rb", [1, 16], F32, kind="ExternalInput").ap()
    d["w1"] = nc.dram_tensor("w1", [16, 8, 128, 2048], F32, kind="ExternalInput").ap()
    d["b1"] = nc.dram_tensor("b1", [16, 1, 2048], F32, kind="ExternalInput").ap()
    d["w2"] = nc.dram_tensor("w2", [16, 16, 128, 1024], F32, kind="ExternalInput").ap()
    d["b2"] = nc.dram_tensor("b2", [16, 1, 1024], F32, kind="ExternalInput").ap()
    d["ident"] = nc.dram_tensor("ident", [128, 128], BF16, kind="ExternalInput").ap()
    d["x_out"] = nc.dram_tensor("x_out", [N, DM], F32, kind="ExternalOutput").ap()

    x_tm = b.sbuf("x_tm", [128, NT * DM], F32)
    yT = b.sbuf("yT", [128, NCH * N], BF16)
    yscr = b.sbuf("yscr", [128, 2, DM], BF16)
    stats = b.sbuf("stats", [128, 8 * NT], F32)
    sqscr = b.sbuf("sqscr", [128, DM], BF16)
    ident = b.sbuf("identt", [128, 128], BF16)
    ones_row = b.sbuf("ones_row", [1, 512], BF16)
    hmid = b.sbuf("hmid", [128, 16 * 1024], BF16)
    comb = b.sbuf("comb", [128, NT * 16], F32)
    rscr = b.sbuf("rscr", [128, 16 * 8], F32)
    rwb = b.sbuf("rwb", [128, 8 * 16], BF16)
    rbr = b.sbuf("rbr", [1, 16], BF16)
    w1b = b.sbuf("w1b", [128, 2, 8 * 512], BF16)
    w2b = b.sbuf("w2b", [128, 2, 16 * 512], BF16)
    b1r = b.sbuf("b1r", [1, 2, 2048], BF16)
    b2r = b.sbuf("b2r", [1, 2, 1024], BF16)

    sy, g = ctx.sync, ctx.gpsimd
    b.s_ldx.inc(sy.dma_start(out=x_tm[:].rearrange("p (t d) -> p t d", t=NT),
                             in_=tm_view(d["x_in"])), 16)
    b.s_ld.inc(g.dma_start(out=ident[:], in_=d["ident"]), 16)
    b.s_ld.inc(g.dma_start(out=rwb[:].rearrange("p (c o) -> p c o", c=8),
                           in_=d["rw"].rearrange("c p o -> p c o")), 16)
    b.s_ld.inc(g.dma_start(out=rbr[:], in_=d["rb"]), 16)
    x_mark = b.s_ldx.mark()
    const_mark = b.s_ld.mark()
    ctx.tensor.wait_ge(b.s_ld.h, const_mark[1])
    ctx.tensor.wait_ge(b.s_ldx.h, x_mark[1])
    b.s_v.inc(ctx.vector.memset(ones_row[:], 1.0))

    cm4, ps4 = b.psum_cm("ps4", [128, 4, 512], F32)
    pool4 = PsumPool(ctx, ps4, 4)
    cmT, psT = b.psum_cm("psT", [128, 2, 1024], BF16)
    psT_pool = PsumPool(ctx, psT, 2)

    ln_mark = emit_ln_to_fm(b, x_tm, yT, ident, psT_pool, yscr, stats, sqscr,
                            deps=[x_mark])

    t_q, v, a = ctx.tensor, ctx.vector, ctx.scalar
    lg = rscr[:, 0:16]
    lmax = rscr[:, 16:17]
    l1 = rscr[:, 32:48]
    ee = rscr[:, 48:64]
    t8 = rscr[:, 64:72]
    s4 = rscr[:, 72:73]
    r4 = rscr[:, 73:74]
    selm = rscr[:, 80:96]
    msel = rscr[:, 96:112]
    junk = rscr[:, 112:116]
    ctx.wait(t_q, ln_mark)
    prev_max_mark = None
    for t in range(NT):
        bank = pool4.acquire(t_q)
        ps = pool4.ap[:, bank, 0:16]
        for c in range(NCH):
            t_q.matmul(ps, yT[:, c * N + t * 128:c * N + (t + 1) * 128],
                       rwb[:, c * 16:(c + 1) * 16], start=(c == 0), stop=False)
        mm = t_q.matmul(ps, ones_row[:, 0:128], rbr[:], start=False, stop=True)
        b.s_t.inc(mm)
        ctx.wait(a, b.s_t.mark())
        if prev_max_mark is not None:
            ctx.wait(a, prev_max_mark)
        b.s_a.inc(a.activation(lg, ps, AF.Copy, bias=0.0, scale=1.0))
        pool4.release(bank, b.s_a.mark())
        ctx.wait(v, b.s_a.mark())
        b.s_v.inc(v.reduce_max(lmax, lg, axis=AX.X))
        prev_max_mark = b.s_v.mark()
        ctx.wait(g, b.s_v.mark())
        b.s_g.inc(g.tensor_scalar(l1, lg, lmax, None, ALU.subtract))
        ctx.wait(a, b.s_g.mark())
        b.s_a.inc(a.activation(ee, l1, AF.Exp))
        ctx.wait(v, b.s_a.mark())
        b.s_v.inc(v.max(t8, ee))
        ctx.wait(a, b.s_v.mark())
        b.s_a.inc(a.activation(junk, t8[:, 0:4], AF.Copy, accum_out=s4))
        ctx.wait(g, b.s_v.mark())
        b.s_g.inc(g.tensor_scalar(selm, ee, t8[:, 3:4], None, ALU.is_ge))
        ctx.wait(v, b.s_a.mark())
        b.s_v.inc(v.reciprocal(r4, s4))
        ctx.wait(v, b.s_g.mark())
        b.s_v.inc(v.tensor_tensor(msel, ee, selm, ALU.mult))
        ctx.wait(g, b.s_v.mark())
        b.s_g.inc(g.tensor_scalar(comb[:, t * 16:(t + 1) * 16], msel, r4,
                                  None, ALU.mult))
    comb_mark = b.s_g.mark()
    ctx.wait(v, comb_mark)

    NH = 2
    HT = NT // NH
    w1_busy = [None, None]
    w2_busy = [None, None]

    b1_busy = [None, None]
    b2_busy = [None, None]

    def load_w1(e, half, s, par, gate):
        if gate is not None:
            ctx.wait(g, gate)
        b.s_ld.inc(g.dma_start(
            out=w1b[:, par, :].rearrange("p (c o) -> p c o", c=8),
            in_=d["w1"][e, :, :, s * 512:(s + 1) * 512].rearrange("c p o -> p c o")), 16)
        if s == 0:
            ep = (half * 16 + e) % 2
            if b1_busy[ep] is not None:
                ctx.wait(g, b1_busy[ep])
            b.s_ld.inc(g.dma_start(out=b1r[:, ep, :], in_=d["b1"][e]), 16)
        return b.s_ld.mark()

    def load_w2(e, half, oh, par, gate):
        if gate is not None:
            ctx.wait(g, gate)
        b.s_ld.inc(g.dma_start(
            out=w2b[:, par, :].rearrange("p (c o) -> p c o", c=16),
            in_=d["w2"][e, :, :, oh * 512:(oh + 1) * 512].rearrange("c p o -> p c o")), 16)
        if oh == 0:
            ep = (half * 16 + e) % 2
            if b2_busy[ep] is not None:
                ctx.wait(g, b2_busy[ep])
            b.s_ld.inc(g.dma_start(out=b2r[:, ep, :], in_=d["b2"][e]), 16)
        return b.s_ld.mark()

    phases = []
    for half in range(NH):
        for e in range(16):
            for s in range(4):
                phases.append(("w1", half, e, s))
            for oh in range(2):
                phases.append(("w2", half, e, oh))
    ph_marks = {}
    ph_par = {}
    w1rr = 0
    w2rr = 0
    loaded = 0

    def issue_load(idx, gate):
        nonlocal w1rr, w2rr
        kind, half, e, s = phases[idx]
        if kind == "w1":
            par = w1rr % 2
            w1rr += 1
            gate2 = w1_busy[par]
            m = load_w1(e, half, s, par, gate2)
        else:
            par = w2rr % 2
            w2rr += 1
            gate2 = w2_busy[par]
            m = load_w2(e, half, s, par, gate2)
        ph_marks[idx] = m
        ph_par[idx] = par

    issue_load(0, None)
    issue_load(1, None)
    nxt = 2
    for idx, (kind, half, e, s) in enumerate(phases):
        par = ph_par[idx]
        ep = (half * 16 + e) % 2
        ctx.wait(t_q, ph_marks[idx])
        if kind == "w1":
            for oc2 in range(4):
                oc = s * 4 + oc2
                for tg in range(2):
                    bank = pool4.acquire(t_q)
                    ps = pool4.ap[:, bank, :]
                    for c in range(NCH):
                        t_q.matmul(ps, w1b[:, par, c * 512 + oc2 * 128:
                                           c * 512 + (oc2 + 1) * 128],
                                   yT[:, c * N + half * 1024 + tg * 512:
                                      c * N + half * 1024 + (tg + 1) * 512],
                                   start=(c == 0), stop=False)
                    mm = t_q.matmul(ps, b1r[:, ep, oc * 128:(oc + 1) * 128],
                                    ones_row[:], start=False, stop=True)
                    b.s_t.inc(mm)
                    ctx.wait(a, b.s_t.mark())
                    ev = a.activation(hmid[:, oc * 1024 + tg * 512:
                                           oc * 1024 + (tg + 1) * 512],
                                      ps, AF.Gelu)
                    b.s_a.inc(ev)
                    pool4.release(bank, b.s_a.mark())
            w1_busy[par] = b.s_t.mark()
            if s == 3:
                b1_busy[ep] = b.s_t.mark()
        else:
            oh = s
            for tl in range(HT):
                T = half * HT + tl
                bank = pool4.acquire(t_q)
                ps = pool4.ap[:, bank, :]
                for c in range(16):
                    t_q.matmul(ps, hmid[:, c * 1024 + tl * 128:c * 1024 + (tl + 1) * 128],
                               w2b[:, par, c * 512:(c + 1) * 512],
                               start=(c == 0), stop=False)
                mm = t_q.matmul(ps, ones_row[:, 0:128],
                                b2r[:, ep, oh * 512:(oh + 1) * 512],
                                start=False, stop=True)
                b.s_t.inc(mm)
                ctx.wait(v, b.s_t.mark())
                dst = x_tm[:, T * DM + oh * 512:T * DM + (oh + 1) * 512]
                ev = v.scalar_tensor_tensor(dst, ps, comb[:, T * 16 + e:T * 16 + e + 1],
                                            dst, ALU.mult, ALU.add)
                b.s_v.inc(ev)
                pool4.release(bank, b.s_v.mark())
            w2_busy[par] = b.s_t.mark()
            if oh == 1:
                b2_busy[ep] = b.s_t.mark()
        if nxt < len(phases):
            issue_load(nxt, None)
            nxt += 1

    fin = b.s_v.mark()
    ctx.wait(ctx.sync, fin)
    b.s_st.inc(ctx.sync.dma_start(out=tm_view(d["x_out"]),
                                  in_=x_tm[:].rearrange("p (t d) -> p t d", t=NT)), 16)
    if dbg:
        dbg_bufs = {"yT": yT, "comb": comb, "hmid": hmid, "rscr": rscr}
        for nm in dbg:
            buf = dbg_bufs[nm]
            if not isinstance(buf, bass.AP):
                buf = buf[:]
            dt_ = BF16 if nm in ("yT", "hmid") else F32
            dd_ = nc.dram_tensor("dbg_" + nm, list(buf.shape), dt_, kind="ExternalOutput").ap()
            b.s_st.inc(ctx.sync.dma_start(out=dd_, in_=buf), 16)
    ctx.sync.wait_ge(b.s_st.h, b.s_st.n)
    cmT.__exit__(None, None, None)
    cm4.__exit__(None, None, None)
    ctx.install()
    return d


def prep_l1_ffn(ins):
    rw = np.asarray(ins["L1_ffn_router_w"])
    rb = np.asarray(ins["L1_ffn_router_b"])
    w1 = np.asarray(ins["L1_ffn_w1"])
    b1 = np.asarray(ins["L1_ffn_b1"])
    w2 = np.asarray(ins["L1_ffn_w2"])
    b2 = np.asarray(ins["L1_ffn_b2"])
    return {
        "rw": chunk_weight(rw),
        "rb": np.ascontiguousarray(rb[None, :].astype(np.float32)),
        "w1": np.ascontiguousarray(w1.reshape(16, 8, 128, 2048).astype(np.float32)),
        "b1": np.ascontiguousarray(b1[:, None, :].astype(np.float32)),
        "w2": np.ascontiguousarray(w2.reshape(16, 16, 128, 1024).astype(np.float32)),
        "b2": np.ascontiguousarray(b2[:, None, :].astype(np.float32)),
        "ident": make_identity_bf16(),
    }


def run_l1_ffn(x_full, ins, dbg=None):
    prep = prep_l1_ffn(ins)
    with ExitStack() as stack:
        nc = bass.Bass("TRN2", target_bir_lowering=False)
        build_l1_ffn(nc, stack, dbg=dbg)
        in_maps = []
        for i in range(8):
            m = dict(prep)
            m["x_in"] = np.ascontiguousarray(x_full[i])
            in_maps.append(m)
        res = run_bass_kernel_spmd(nc, in_maps, core_ids=list(range(8)))
    out = np.stack([res.results[i]["x_out"] for i in range(8)], axis=0)
    if dbg:
        return out, res.results
    return out


def emit_rope_interleaved(b, xtm, ctab, stab, scr, nt=NT, nh=2, hw=128):
    ctx, v, g = b.ctx, b.ctx.vector, b.ctx.gpsimd
    nj = 32
    for t in range(nt):
        tile = xtm[:, t * hw:(t + 1) * hw].rearrange("p (h j two) -> p h j two",
                                                     h=nh, two=2)
        qe = tile[:, :, :, 0]
        qo = tile[:, :, :, 1]
        ct = ctab[:, t * nj:(t + 1) * nj].rearrange("p (o j) -> p o j", o=1) \
            .broadcast_to((128, nh, nj))
        st = stab[:, t * nj:(t + 1) * nj].rearrange("p (o j) -> p o j", o=1) \
            .broadcast_to((128, nh, nj))
        sv = scr[:].rearrange("p (m h j) -> p m h j", m=4, h=nh)
        m1, m2, m3, m4 = sv[:, 0], sv[:, 1], sv[:, 2], sv[:, 3]
        g.tensor_tensor(m1, qe, ct, ALU.mult)
        b.s_g.inc(g.tensor_tensor(m3, qe, st, ALU.mult))
        g1 = b.s_g.mark()
        ctx.wait(v, g1)
        v.tensor_tensor(m2, qo, st, ALU.mult)
        b.s_v.inc(v.tensor_tensor(m4, qo, ct, ALU.mult))
        v1 = b.s_v.mark()
        ctx.wait(g, v1)
        b.s_g.inc(g.tensor_tensor(qe, m1, m2, ALU.subtract))
        g2 = b.s_g.mark()
        ctx.wait(v, g2)
        b.s_v.inc(v.tensor_tensor(qo, m3, m4, ALU.add))
    return b.s_v.mark(), b.s_g.mark()


def build_l0_attn_b(nc, stack, dbg=None):
    ctx = Ctx(nc, stack)
    b = B(ctx)
    d = {}
    d["x_in"] = nc.dram_tensor("x_in", [N, DM], F32, kind="ExternalInput").ap()
    d["wq"] = nc.dram_tensor("wq", [8, 128, DM], F32, kind="ExternalInput").ap()
    d["wk"] = nc.dram_tensor("wk", [8, 128, DM], F32, kind="ExternalInput").ap()
    d["wv"] = nc.dram_tensor("wv", [8, 128, DM], F32, kind="ExternalInput").ap()
    d["wo"] = nc.dram_tensor("wo", [8, 128, DM], F32, kind="ExternalInput").ap()
    d["alpha"] = nc.dram_tensor("alpha", [1, 1], F32, kind="ExternalInput").ap()
    d["ident"] = nc.dram_tensor("ident", [128, 128], BF16, kind="ExternalInput").ap()
    d["cmaskf"] = nc.dram_tensor("cmaskf", [128, 128], F32, kind="ExternalInput").ap()
    d["ctab"] = nc.dram_tensor("ctab", [128, NT * 32], F32, kind="ExternalInput").ap()
    d["stab"] = nc.dram_tensor("stab", [128, NT * 32], F32, kind="ExternalInput").ap()
    d["halfneg"] = nc.dram_tensor("halfneg", [128, 512], BF16, kind="ExternalInput").ap()
    d["identf"] = nc.dram_tensor("identf", [128, 128], F32, kind="ExternalInput").ap()
    d["onescol"] = nc.dram_tensor("onescol", [1, 128], F32, kind="ExternalInput").ap()
    d["x_out"] = nc.dram_tensor("x_out", [N, DM], F32, kind="ExternalOutput").ap()

    HD = 64

    x_tm = b.sbuf("x_tm", [128, NT * DM], F32)
    yT = b.sbuf("yT", [128, NCH * N], BF16)
    yscr = b.sbuf("yscr", [128, 2, DM], BF16)
    stats = b.sbuf("stats", [128, 8 * NT], F32)
    sqscr = b.sbuf("sqscr", [128, DM], BF16)
    ident = b.sbuf("identt", [128, 128], BF16)
    cmaskf = b.sbuf("cmaskf_t", [128, 128], F32)
    ctab = b.sbuf("ctab_t", [128, NT * 32], F32)
    stab = b.sbuf("stab_t", [128, NT * 32], F32)
    halfneg = b.sbuf("halfneg_t", [128, 512], BF16)
    onescol = b.sbuf("onescol_t", [1, 128], F32)
    alpha_t = b.sbuf("alpha_t", [1, 1], F32)
    svalt = b.sbuf("svalt", [1, 1], F32)
    scale_col = b.sbuf("scale_col", [128, 1], F32)
    attnout = b.sbuf("attnout", [128, NCH * N], BF16)
    qktm = b.sbuf("qktm", [128, NT * 128], F32)
    vtm = b.sbuf("vtm", [128, NT * 128], BF16)
    qfm = b.sbuf("qfm", [128, N], F32)
    kfm = b.sbuf("kfm", [128, N], F32)
    identf = b.sbuf("identf_t", [128, 128], F32)
    qrsq = b.sbuf("qrsq", [128, N], BF16)
    krsq = b.sbuf("krsq", [128, N], BF16)
    ropescr = b.sbuf("ropescr", [128, 4 * 2 * 32], F32)
    rowaux = b.sbuf("rowaux", [128, 8], F32)
    eT = b.sbuf("eT", [128, 2, 128], BF16)
    whb = b.sbuf("whb", [128, 2, 8 * 128], BF16)
    cm_d2p = nc.sbuf_tensor("d2p", [128, N], F32)
    d2p = cm_d2p.__enter__()
    cm_sqr = nc.sbuf_tensor("sqr", [128, N], BF16)
    sqr = cm_sqr.__enter__()
    cm_s1 = nc.sbuf_tensor("s1", [128, N], F32)
    s1 = cm_s1.__enter__()
    cm_eb = nc.sbuf_tensor("eb", [128, N], BF16)
    eb = cm_eb.__enter__()

    sy, g = ctx.sync, ctx.gpsimd
    b.s_ldx.inc(sy.dma_start(out=x_tm[:].rearrange("p (t d) -> p t d", t=NT),
                             in_=tm_view(d["x_in"])), 16)
    for src, dst in ((d["cmaskf"], cmaskf), (d["ctab"], ctab), (d["stab"], stab),
                     (d["onescol"], onescol), (d["alpha"], alpha_t)):
        b.s_ldx.inc(sy.dma_start(out=dst[:], in_=src), 16)
    b.s_ld.inc(g.dma_start(out=ident[:], in_=d["ident"]), 16)
    b.s_ld.inc(g.dma_start(out=halfneg[:], in_=d["halfneg"]), 16)
    b.s_ld.inc(g.dma_start(out=identf[:], in_=d["identf"]), 16)
    x_mark = b.s_ldx.mark()
    const_mark = b.s_ld.mark()
    ctx.tensor.wait_ge(b.s_ld.h, const_mark[1])
    ctx.tensor.wait_ge(b.s_ldx.h, x_mark[1])

    cm4, ps4 = b.psum_cm("ps4", [128, 4, 512], F32)
    pool4 = PsumPool(ctx, ps4, 4)
    cmT, psT = b.psum_cm("psT", [128, 2, 1024], BF16)
    psT_pool = PsumPool(ctx, psT, 2)
    cmAV, psAV = b.psum_cm("psAV", [128, 512], F32)

    t_q, v, a = ctx.tensor, ctx.vector, ctx.scalar
    lnbase = float(np.log(np.sqrt(64.0) / np.log(65.0)))
    ctx.wait(a, x_mark)
    b.s_a.inc(a.activation(svalt[:], alpha_t[:], AF.Exp, bias=0.0, scale=lnbase))
    ctx.wait(t_q, b.s_a.mark())
    mmq = t_q.matmul(psAV[0:128, 0:1], onescol[:], svalt[:], start=True, stop=True)
    b.s_t.inc(mmq)
    ctx.wait(a, b.s_t.mark())
    b.s_a.inc(a.activation(scale_col[:], psAV[0:128, 0:1], AF.Copy, bias=0.0, scale=1.0))
    scale_mark = b.s_a.mark()

    ln_mark = emit_ln_to_fm(b, x_tm, yT, ident, psT_pool, yscr, stats, sqscr,
                            deps=[x_mark])

    wpar_state = {"rr": 0}
    wh_busy = [None, None]

    def load_hp_w(dd, hp, after_mark):
        par = wpar_state["rr"] % 2
        wpar_state["rr"] += 1
        if wh_busy[par] is not None:
            ctx.wait(g, wh_busy[par])
        if after_mark is not None:
            ctx.wait(g, after_mark)
        b.s_ld.inc(g.dma_start(out=whb[:, par, :].rearrange("p (c o) -> p c o", c=8),
                               in_=dd[:, :, hp * 128:(hp + 1) * 128]
                               .rearrange("c p o -> p c o")), 16)
        return par, b.s_ld.mark()

    wq_next = load_hp_w(d["wv"], 0, None)
    gates = {"pool_s1": None, "transp": None, "avem": None}
    for hp in range(8):
        vpar, vmark = wq_next
        qpar_next = load_hp_w(d["wq"], hp, None)
        vm = emit_proj_tm(b, vtm, 128, yT,
                          lambda c: whb[:, vpar, c * 128:(c + 1) * 128],
                          NCH, 128, pool4, None, ready=[ln_mark, vmark])
        wh_busy[vpar] = b.s_t.mark()
        for qk_i, dst in ((0, qfm), (1, kfm)):
            ppar, pmark = qpar_next
            if qk_i == 0:
                qpar_next = load_hp_w(d["wk"], hp, None)
            elif hp + 1 < 8:
                qpar_next = load_hp_w(d["wv"], hp + 1, None)
            pm = emit_proj_tm(b, qktm, 128, yT,
                              lambda c, ppar=ppar: whb[:, ppar, c * 128:(c + 1) * 128],
                              NCH, 128, pool4, None, ready=[pmark])
            wh_busy[ppar] = b.s_t.mark()
            ctx.wait(g, pm)
            ctx.wait(v, pm)
            emit_rope_interleaved(b, qktm, ctab, stab, ropescr)
            rv, rg = b.s_v.mark(), b.s_g.mark()
            ctx.wait(t_q, rv)
            ctx.wait(t_q, rg)
            for t in range(NT):
                bank = pool4.acquire(t_q)
                tr = t_q.transpose(pool4.ap[:, bank, 0:128],
                                   qktm[:, t * 128:(t + 1) * 128], identf[:])
                b.s_t.inc(tr)
                ctx.wait(a, b.s_t.mark())
                ev = a.activation(dst[:, t * 128:(t + 1) * 128],
                                  pool4.ap[:, bank, 0:128],
                                  AF.Copy, bias=0.0, scale=1.0)
                b.s_a.inc(ev)
                pool4.release(bank, b.s_a.mark())
            ctx.wait(v, b.s_t.mark())
        wq_next = qpar_next
        tfm_mark = b.s_a.mark()
        ctx.wait(g, tfm_mark)
        ctx.wait(g, b.s_t.mark())
        g.tensor_tensor(qrsq[:], qfm[:], qfm[:], ALU.mult)
        b.s_g.inc(g.tensor_tensor(krsq[:], kfm[:], kfm[:], ALU.mult))
        sq_mark = b.s_g.mark()
        ctx.wait(t_q, sq_mark)
        for hi in range(2):
            h0 = 64 * hi
            for qt in range(NT):
                kmax = (qt + 1) * 128
                ngr = (kmax + 511) // 512
                first_evac = True
                for gr in range(ngr):
                    k0 = gr * 512
                    kw = min(512, kmax - k0)
                    bA = pool4.acquire(t_q)
                    mm = t_q.matmul(pool4.ap[:, bA, 0:kw],
                                    qfm[h0:h0 + 64, qt * 128:(qt + 1) * 128],
                                    kfm[h0:h0 + 64, k0:k0 + kw], start=True, stop=True)
                    b.s_t.inc(mm)
                    bB = pool4.acquire(t_q)
                    t_q.matmul(pool4.ap[:, bB, 0:kw],
                               halfneg[h0:h0 + 64, 0:128],
                               krsq[h0:h0 + 64, k0:k0 + kw], start=True, stop=False)
                    mm = t_q.matmul(pool4.ap[:, bB, 0:kw],
                                    qrsq[h0:h0 + 64, qt * 128:(qt + 1) * 128],
                                    halfneg[h0:h0 + 64, 0:kw], start=False, stop=True)
                    b.s_t.inc(mm)
                    mB = b.s_t.mark()
                    if first_evac and gates["pool_s1"] is not None:
                        ctx.wait(a, gates["pool_s1"])
                        ctx.wait(v, gates["pool_s1"])
                    first_evac = False
                    ctx.wait(a, mB)
                    a.activation(d2p[:, k0:k0 + kw], pool4.ap[:, bB, 0:kw],
                                 AF.Copy, bias=0.0, scale=-2.0)
                    e2 = a.activation(sqr[:, k0:k0 + kw], pool4.ap[:, bA, 0:kw],
                                      AF.Square, bias=0.0, scale=1.0)
                    b.s_a.inc(e2)
                    am = b.s_a.mark()
                    pool4.release(bB, am)
                    ctx.wait(v, am)
                    dv = v.scalar_tensor_tensor(d2p[:, k0:k0 + kw],
                                                pool4.ap[:, bA, 0:kw], -2.0,
                                                d2p[:, k0:k0 + kw],
                                                ALU.mult, ALU.add)
                    b.s_v.inc(dv)
                    pool4.release(bA, b.s_v.mark())
                evac_mark = b.s_v.mark()
                ctx.wait(g, evac_mark)
                ctx.wait(g, b.s_a.mark())
                b.s_g.inc(g.tensor_scalar(d2p[:, 0:kmax], d2p[:, 0:kmax],
                                          1e-5, None, ALU.add))
                ctx.wait(v, b.s_g.mark())
                b.s_v.inc(v.reciprocal(d2p[:, 0:kmax], d2p[:, 0:kmax]))
                ctx.wait(g, b.s_v.mark())
                b.s_g.inc(g.tensor_tensor(s1[:, 0:kmax], sqr[:, 0:kmax],
                                          d2p[:, 0:kmax], ALU.mult))
                gates["pool_s1"] = b.s_g.mark()
                ctx.wait(v, b.s_g.mark())
                pr = (hi * NT + qt) % 2
                b.s_v.inc(v.reduce_max(rowaux[:, pr:pr + 1], s1[:, 0:kmax], axis=AX.X))
                ctx.wait(g, b.s_v.mark())
                b.s_g.inc(g.tensor_scalar(s1[:, 0:kmax], s1[:, 0:kmax],
                                          rowaux[:, pr:pr + 1], None, ALU.subtract))
                ctx.wait(v, b.s_g.mark())
                b.s_v.inc(v.tensor_tensor(s1[:, kmax - 128:kmax],
                                          s1[:, kmax - 128:kmax], cmaskf[:], ALU.add))
                ctx.wait(a, b.s_v.mark())
                ctx.wait(a, scale_mark)
                if gates["transp"] is not None:
                    ctx.wait(a, gates["transp"])
                b.s_a.inc(a.activation(eb[:, 0:kmax], s1[:, 0:kmax], AF.Exp,
                                       bias=0.0, scale=scale_col[:, 0:1],
                                       accum_out=rowaux[:, 2 + pr:3 + pr]))
                ctx.wait(v, b.s_a.mark())
                b.s_v.inc(v.reciprocal(rowaux[:, 4 + pr:5 + pr],
                                       rowaux[:, 2 + pr:3 + pr]))
                ctx.wait(g, b.s_v.mark())
                b.s_g.inc(g.tensor_scalar(eb[:, 0:kmax], eb[:, 0:kmax],
                                          rowaux[:, 4 + pr:5 + pr], None, ALU.mult))
                en_mark = b.s_g.mark()
                ctx.wait(t_q, en_mark)
                av_marks = {}
                for kt in range(qt + 1):
                    bankT = psT_pool.acquire(t_q)
                    tr = t_q.transpose(psT[:, bankT, 0:128],
                                       eb[:, kt * 128:(kt + 1) * 128], ident[:])
                    b.s_t.inc(tr)
                    trm = b.s_t.mark()
                    ctx.wait(v, trm)
                    if kt >= 2:
                        ctx.wait(v, av_marks[kt - 2])
                    b.s_v.inc(v.tensor_copy(eT[:, kt % 2, :], psT[:, bankT, 0:128]))
                    evm = b.s_v.mark()
                    psT_pool.release(bankT, evm)
                    ctx.wait(t_q, evm)
                    mm = t_q.matmul(psAV[0:64, 0:128],
                                    vtm[:, kt * 128 + hi * 64:kt * 128 + hi * 64 + 64],
                                    eT[:, kt % 2, :], start=(kt == 0), stop=(kt == qt))
                    b.s_t.inc(mm)
                    av_marks[kt] = b.s_t.mark()
                gates["transp"] = b.s_t.mark()
                ctx.wait(v, b.s_t.mark())
                dst = attnout[hi * 64:hi * 64 + 64,
                              hp * N + qt * 128:hp * N + (qt + 1) * 128]
                b.s_v.inc(v.tensor_copy(dst, psAV[0:64, 0:128]))
                avem = b.s_v.mark()
                ctx.wait(t_q, avem)

    cm_eb.__exit__(None, None, None)
    cm_s1.__exit__(None, None, None)
    cm_sqr.__exit__(None, None, None)
    cm_d2p.__exit__(None, None, None)
    wob = b.sbuf("wob", [128, 8 * DM], BF16)
    for c in range(8):
        b.s_ld.inc(g.dma_start(out=wob[:, c * DM:(c + 1) * DM], in_=d["wo"][c]), 16)
    ctx.wait(t_q, b.s_ld.mark())
    fin = emit_proj_tm(b, x_tm, DM, attnout,
                       lambda c: wob[:, c * DM:(c + 1) * DM],
                       8, DM, pool4, None, ready=[b.s_v.mark()], add_to=x_tm)
    ctx.wait(ctx.sync, fin)
    b.s_st.inc(ctx.sync.dma_start(out=tm_view(d["x_out"]),
                                  in_=x_tm[:].rearrange("p (t d) -> p t d", t=NT)), 16)
    if dbg:
        dbg_bufs = {"yT": yT, "attnout": attnout, "qktm": qktm,
                    "vtm": vtm, "qfm": qfm, "kfm": kfm, "qrsq": qrsq,
                    "krsq": krsq, "scale_col": scale_col}
        for nm in dbg:
            buf = dbg_bufs[nm]
            if not isinstance(buf, bass.AP):
                buf = buf[:]
            dt_ = F32 if nm in ("qktm", "scale_col", "qfm", "kfm") else BF16
            dd_ = nc.dram_tensor("dbg_" + nm, list(buf.shape), dt_,
                                 kind="ExternalOutput").ap()
            b.s_st.inc(ctx.sync.dma_start(out=dd_, in_=buf), 16)
    ctx.sync.wait_ge(b.s_st.h, b.s_st.n)
    cmAV.__exit__(None, None, None)
    cmT.__exit__(None, None, None)
    cm4.__exit__(None, None, None)
    ctx.install()
    return d


def prep_l0_attn_b(ins):
    Wq = np.asarray(ins["L0_attn_b_Wq"])
    Wk = np.asarray(ins["L0_attn_b_Wk"])
    Wv = np.asarray(ins["L0_attn_b_Wv"])
    Wo = np.asarray(ins["L0_attn_b_Wo"])
    alpha = np.asarray(ins["L0_attn_b_alpha"]).reshape(1, 1).astype(np.float32)
    HD = 64
    inv_freq = 1.0 / (10000.0 ** (np.arange(0, HD, 2)[: HD // 2] / HD))
    t = np.arange(N)
    freqs = np.outer(t, inv_freq)
    cost = np.cos(freqs).astype(np.float32)
    sint = np.sin(freqs).astype(np.float32)
    ctab = np.zeros((128, NT * 32), np.float32)
    stab = np.zeros((128, NT * 32), np.float32)
    for tt in range(NT):
        ctab[:, tt * 32:(tt + 1) * 32] = cost[tt * 128:(tt + 1) * 128]
        stab[:, tt * 32:(tt + 1) * 32] = sint[tt * 128:(tt + 1) * 128]
    cmaskf = np.where(np.arange(128)[:, None] >= np.arange(128)[None, :],
                      np.float32(0.0), np.float32(-1e30)).astype(np.float32)
    return {"wq": chunk_weight(Wq), "wk": chunk_weight(Wk), "wv": chunk_weight(Wv),
            "wo": chunk_weight(Wo), "alpha": alpha,
            "ident": make_identity_bf16(), "identf": np.eye(128, dtype=np.float32),
            "cmaskf": cmaskf,
            "ctab": ctab, "stab": stab,
            "halfneg": np.full((128, 512), -0.5, np.float32).astype(BF),
            "onescol": np.ones((1, 128), np.float32)}


def run_l0_attn_b(x_full, ins, dbg=None):
    prep = prep_l0_attn_b(ins)
    with ExitStack() as stack:
        nc = bass.Bass("TRN2", target_bir_lowering=False)
        build_l0_attn_b(nc, stack, dbg=dbg)
        in_maps = []
        for i in range(8):
            m = dict(prep)
            m["x_in"] = np.ascontiguousarray(x_full[i])
            in_maps.append(m)
        res = run_bass_kernel_spmd(nc, in_maps, core_ids=list(range(8)))
    out = np.stack([res.results[i]["x_out"] for i in range(8)], axis=0)
    if dbg:
        return out, res.results
    return out


def build_l0_attn_a(nc, stack, dbg=None):
    ctx = Ctx(nc, stack)
    b = B(ctx)
    d = {}
    d["x_in"] = nc.dram_tensor("x_in", [N, DM], F32, kind="ExternalInput").ap()
    for nm in ("wq", "wk", "wv", "wo"):
        d[nm] = nc.dram_tensor(nm, [8, 128, DM], F32, kind="ExternalInput").ap()
    d["bq_r"] = nc.dram_tensor("bq_r", [1, DM], F32, kind="ExternalInput").ap()
    d["bk_r"] = nc.dram_tensor("bk_r", [1, DM], F32, kind="ExternalInput").ap()
    d["bv_r"] = nc.dram_tensor("bv_r", [1, DM], F32, kind="ExternalInput").ap()
    d["bo_r"] = nc.dram_tensor("bo_r", [1, DM], F32, kind="ExternalInput").ap()
    d["ident"] = nc.dram_tensor("ident", [128, 128], BF16, kind="ExternalInput").ap()
    d["sincol"] = nc.dram_tensor("sincol", [128, NT], F32, kind="ExternalInput").ap()
    d["coscol"] = nc.dram_tensor("coscol", [128, NT], F32, kind="ExternalInput").ap()
    d["x_out"] = nc.dram_tensor("x_out", [N, DM], F32, kind="ExternalOutput").ap()

    x_tm = b.sbuf("x_tm", [128, NT * DM], F32)
    yT = b.sbuf("yT", [128, NCH * N], BF16)
    yscr = b.sbuf("yscr", [128, 2, DM], BF16)
    stats = b.sbuf("stats", [128, 8 * NT], F32)
    sqscr = b.sbuf("sqscr", [128, DM], BF16)
    ident = b.sbuf("identt", [128, 128], BF16)
    ones_row = b.sbuf("ones_row", [1, 512], BF16)
    onescolb = b.sbuf("onescolb", [128, 1], BF16)
    sincol = b.sbuf("sincol_t", [128, NT], F32)
    coscol = b.sbuf("coscol_t", [128, NT], F32)
    attnout = b.sbuf("attnout", [128, NCH * N], BF16)
    whb = b.sbuf("whb", [128, 2, 8 * 128], BF16)
    biasr = b.sbuf("biasr", [1, 4 * DM], BF16)
    smscr = b.sbuf("smscr", [128, 16], F32)
    cms = []

    def scoped(name, shape, dt_):
        cm = nc.sbuf_tensor(name, shape, dt_)
        cms.append(cm)
        return cm.__enter__()

    q_tm = scoped("q_tm", [128, NT * 128], F32)
    k_tm = scoped("k_tm", [128, NT * 128], F32)
    v_tm = scoped("v_tm", [128, NT * 128], BF16)
    kf = scoped("kf", [128, 4, NT * 128], BF16)
    qfF = scoped("qfF", [128, 4, N], BF16)
    fscr = scoped("fscr", [128, 2, 128], BF16)
    kvaug = scoped("kvaug", [128, 4 * 65], BF16)
    zall = scoped("zall", [128, 512], F32)
    zrb = scoped("zrb", [128, 512], BF16)
    onesb128 = scoped("onesb128", [128, 64], BF16)
    zbsb = scoped("zbsb", [128, 512], BF16)
    t1a = scoped("t1a", [128, 512], BF16)
    t1b = scoped("t1b", [128, 512], BF16)

    sy, g = ctx.sync, ctx.gpsimd
    b.s_ldx.inc(sy.dma_start(out=x_tm[:].rearrange("p (t d) -> p t d", t=NT),
                             in_=tm_view(d["x_in"])), 16)
    b.s_ldx.inc(sy.dma_start(out=sincol[:], in_=d["sincol"]), 16)
    b.s_ldx.inc(sy.dma_start(out=coscol[:], in_=d["coscol"]), 16)
    b.s_ld.inc(g.dma_start(out=ident[:], in_=d["ident"]), 16)
    for i, nm in enumerate(("bq_r", "bk_r", "bv_r", "bo_r")):
        b.s_ld.inc(g.dma_start(out=biasr[:, i * DM:(i + 1) * DM], in_=d[nm]), 16)
    x_mark = b.s_ldx.mark()
    const_mark = b.s_ld.mark()
    ctx.tensor.wait_ge(b.s_ld.h, const_mark[1])
    ctx.tensor.wait_ge(b.s_ldx.h, x_mark[1])
    b.s_v.inc(ctx.vector.memset(ones_row[:], 1.0))
    b.s_v.inc(ctx.vector.memset(onescolb[:], 1.0))
    b.s_v.inc(ctx.vector.memset(onesb128[:], 1.0))

    cm4, ps4 = b.psum_cm("ps4", [128, 4, 512], F32)
    pool4 = PsumPool(ctx, ps4, 4)
    cmT, psT = b.psum_cm("psT", [128, 2, 1024], BF16)
    psT_pool = PsumPool(ctx, psT, 2)

    ln_mark = emit_ln_to_fm(b, x_tm, yT, ident, psT_pool, yscr, stats, sqscr,
                            deps=[x_mark])

    t_q, v, a = ctx.tensor, ctx.vector, ctx.scalar

    wpar_state = {"rr": 0}
    wh_busy = [None, None]

    def load_hp_w(dd, hp):
        par = wpar_state["rr"] % 2
        wpar_state["rr"] += 1
        if wh_busy[par] is not None:
            ctx.wait(g, wh_busy[par])
        b.s_ld.inc(g.dma_start(out=whb[:, par, :].rearrange("p (c o) -> p c o", c=8),
                               in_=dd[:, :, hp * 128:(hp + 1) * 128]
                               .rearrange("c p o -> p c o")), 16)
        return par, b.s_ld.mark()

    def softmax_headdim(buf, hi_count=2):
        for t in range(NT):
            for hi in range(hi_count):
                sl = buf[:, t * 128 + hi * 64:t * 128 + (hi + 1) * 64]
                b.s_v.inc(v.reduce_max(smscr[:, 0:1], sl, axis=AX.X))
                ctx.wait(g, b.s_v.mark())
                b.s_g.inc(g.tensor_scalar(sl, sl, smscr[:, 0:1], None, ALU.subtract))
                ctx.wait(a, b.s_g.mark())
                b.s_a.inc(a.activation(sl, sl, AF.Exp, accum_out=smscr[:, 1:2]))
                ctx.wait(v, b.s_a.mark())
                b.s_v.inc(v.reciprocal(smscr[:, 2:3], smscr[:, 1:2]))
                ctx.wait(g, b.s_v.mark())
                b.s_g.inc(g.tensor_scalar(sl, sl, smscr[:, 2:3], None, ALU.mult))
                ctx.wait(v, b.s_g.mark())

    nxt_w = load_hp_w(d["wq"], 0)
    for hp in range(8):
        for wi, (dst, bofs) in enumerate(((q_tm, 0), (k_tm, DM), (v_tm, 2 * DM))):
            par, wm = nxt_w
            if wi == 0:
                nxt_w = load_hp_w(d["wk"], hp)
            elif wi == 1:
                nxt_w = load_hp_w(d["wv"], hp)
            elif hp + 1 < 8:
                nxt_w = load_hp_w(d["wq"], hp + 1)
            emit_proj_tm(b, dst, 128, yT,
                         lambda c, par=par: whb[:, par, c * 128:(c + 1) * 128],
                         NCH, 128, pool4, ones_row,
                         ready=[ln_mark, wm],
                         bias_row=biasr[:, bofs + hp * 128:bofs + (hp + 1) * 128])
            wh_busy[par] = b.s_t.mark()
        softmax_headdim(q_tm)
        softmax_headdim(k_tm)
        sm_mark_v, sm_mark_g, sm_mark_a = b.s_v.mark(), b.s_g.mark(), b.s_a.mark()
        ctx.wait(v, sm_mark_g)
        ctx.wait(g, sm_mark_v)
        for t in range(NT):
            ks = kf[:, 0, t * 128:(t + 1) * 128]
            kc = kf[:, 1, t * 128:(t + 1) * 128]
            cks = kf[:, 2, t * 128:(t + 1) * 128]
            ckc = kf[:, 3, t * 128:(t + 1) * 128]
            sl = k_tm[:, t * 128:(t + 1) * 128]
            v.tensor_scalar(ks, sl, sincol[:, t:t + 1], None, ALU.mult)
            b.s_v.inc(v.tensor_scalar(kc, sl, coscol[:, t:t + 1], None, ALU.mult))
            ctx.wait(g, b.s_v.mark())
            g.tensor_scalar(cks, ks, -1.0, sincol[:, t:t + 1], ALU.mult, ALU.add)
            b.s_g.inc(g.tensor_scalar(ckc, kc, -1.0, coscol[:, t:t + 1],
                                      ALU.mult, ALU.add))
        kf_mark = b.s_g.mark()
        for fi in range(4):
            for t in range(NT):
                sl = q_tm[:, t * 128:(t + 1) * 128]
                dstf = fscr[:, t % 2, :]
                scol = sincol if fi in (0, 2) else coscol
                if fi < 2:
                    r = g.tensor_scalar(dstf, sl, scol[:, t:t + 1], None, ALU.mult)
                    b.s_g.inc(r)
                    fm = b.s_g.mark()
                else:
                    r = g.tensor_scalar(dstf, sl, scol[:, t:t + 1], None, ALU.mult)
                    b.s_g.inc(r)
                    ctx.wait(v, b.s_g.mark())
                    r = v.tensor_scalar(dstf, dstf, -1.0, scol[:, t:t + 1],
                                        ALU.mult, ALU.add)
                    b.s_v.inc(r)
                    fm = b.s_v.mark()
                ctx.wait(t_q, fm)
                bank = psT_pool.acquire(t_q)
                tr = t_q.transpose(psT[:, bank, 0:128], dstf, ident[:])
                b.s_t.inc(tr)
                ctx.wait(a, b.s_t.mark())
                ev = a.activation(qfF[:, fi, t * 128:(t + 1) * 128],
                                  psT[:, bank, 0:128], AF.Copy, bias=0.0, scale=1.0)
                b.s_a.inc(ev)
                psT_pool.release(bank, b.s_a.mark())
                ctx.wait(g, b.s_t.mark())
                ctx.wait(v, b.s_t.mark())
        qf_mark = b.s_a.mark()
        ctx.wait(t_q, kf_mark)
        ctx.wait(t_q, qf_mark)
        for br in range(2):
            for trig in range(2):
                fi = br * 2 + trig
                for hi in range(2):
                    slot = br * 2 + trig
                    h0 = hi * 64
                    bank = pool4.acquire(t_q)
                    for t in range(NT):
                        t_q.matmul(ps4[h0:h0 + 64, bank, 0:64],
                                   kf[:, fi, t * 128 + h0:t * 128 + h0 + 64],
                                   v_tm[:, t * 128 + h0:t * 128 + h0 + 64],
                                   start=(t == 0), stop=(t == NT - 1))
                        mm = t_q.matmul(ps4[h0:h0 + 64, bank, 64:65],
                                        kf[:, fi, t * 128 + h0:t * 128 + h0 + 64],
                                        onescolb[:], start=(t == 0), stop=(t == NT - 1))
                    b.s_t.inc(mm)
                    ctx.wait(a, b.s_t.mark())
                    ev = a.activation(kvaug[h0:h0 + 64, slot * 65:(slot + 1) * 65],
                                      ps4[h0:h0 + 64, bank, 0:65], AF.Copy,
                                      bias=0.0, scale=1.0)
                    b.s_a.inc(ev)
                    pool4.release(bank, b.s_a.mark())
        kv_mark = b.s_a.mark()
        ctx.wait(t_q, kv_mark)
        prev_t1 = {0: None, 1: None}
        for hi in range(2):
            h0 = hi * 64
            for tg in range(4):
                for br, t1buf in ((0, t1a), (1, t1b)):
                    bank = pool4.acquire(t_q)
                    for trig in range(2):
                        slot = br * 2 + trig
                        mm = t_q.matmul(ps4[0:65, bank, 0:512],
                                        kvaug[h0:h0 + 64, slot * 65:(slot + 1) * 65],
                                        qfF[h0:h0 + 64, br * 2 + trig,
                                            tg * 512:(tg + 1) * 512],
                                        start=(trig == 0), stop=(trig == 1))
                    b.s_t.inc(mm)
                    num_mark = b.s_t.mark()
                    ctx.wait(a, num_mark)
                    if prev_t1[br] is not None:
                        ctx.wait(a, prev_t1[br])
                    ev = a.activation(zall[64:65, 0:512],
                                      ps4[64:65, bank, 0:512], AF.Copy,
                                      bias=0.0, scale=1.0)
                    b.s_a.inc(ev)
                    ctx.wait(g, b.s_a.mark())
                    r = g.tensor_scalar(zall[64:65, 0:512], zall[64:65, 0:512],
                                        1e-6, None, ALU.max)
                    b.s_g.inc(r)
                    ctx.wait(v, b.s_g.mark())
                    b.s_v.inc(v.reciprocal(zall[64:65, 0:512], zall[64:65, 0:512]))
                    ctx.wait(g, b.s_v.mark())
                    b.s_g.inc(g.tensor_scalar(zrb[64:65, 0:512], zall[64:65, 0:512],
                                              1.0, None, ALU.mult))
                    zr_mark = b.s_g.mark()
                    ctx.wait(t_q, zr_mark)
                    bank2 = pool4.acquire(t_q)
                    mm = t_q.matmul(ps4[0:64, bank2, 0:512], onesb128[64:65, 0:64],
                                    zrb[64:65, 0:512], start=True, stop=True)
                    b.s_t.inc(mm)
                    ctx.wait(a, b.s_t.mark())
                    ev = a.activation(zbsb[0:64, :], ps4[0:64, bank2, 0:512], AF.Copy,
                                      bias=0.0, scale=1.0)
                    b.s_a.inc(ev)
                    pool4.release(bank2, b.s_a.mark())
                    ctx.wait(v, b.s_a.mark())
                    tt = v.tensor_tensor(t1buf[0:64, :], ps4[0:64, bank, 0:512],
                                         zbsb[0:64, :], ALU.mult)
                    b.s_v.inc(tt)
                    prev_t1[br] = b.s_v.mark()
                    pool4.release(bank, b.s_v.mark())
                ctx.wait(g, b.s_v.mark())
                r = g.tensor_tensor(attnout[hi * 64:hi * 64 + 64,
                                            hp * N + tg * 512:hp * N + (tg + 1) * 512],
                                    t1a[0:64, :], t1b[0:64, :], ALU.add)
                b.s_g.inc(r)
        ctx.wait(v, b.s_g.mark())

    for cm in reversed(cms):
        cm.__exit__(None, None, None)
    d["attn_fm"] = nc.dram_tensor("attn_fm", [128, NCH * N], BF16,
                                  kind="ExternalOutput").ap()
    ctx.wait(ctx.sync, b.s_g.mark())
    b.s_st.inc(ctx.sync.dma_start(out=d["attn_fm"], in_=attnout[:]), 16)
    if dbg:
        dbg_bufs = {"yT": yT, "attnout": attnout, "q_tm": q_tm, "k_tm": k_tm,
                    "v_tm": v_tm, "kf": kf[:, 0, :], "qfF": qfF[:, 0, :],
                    "kvaug": kvaug}
        for nm in dbg:
            buf = dbg_bufs[nm]
            if not isinstance(buf, bass.AP):
                buf = buf[:]
            dd_ = nc.dram_tensor("dbg_" + nm, list(buf.shape), BF16,
                                 kind="ExternalOutput").ap()
            b.s_st.inc(ctx.sync.dma_start(out=dd_, in_=buf), 16)
    ctx.sync.wait_ge(b.s_st.h, b.s_st.n)
    cmT.__exit__(None, None, None)
    cm4.__exit__(None, None, None)
    ctx.install()
    return d


def build_l0_wo(nc, stack):
    ctx = Ctx(nc, stack)
    b = B(ctx)
    d = {}
    d["x_in"] = nc.dram_tensor("x_in", [N, DM], F32, kind="ExternalInput").ap()
    d["p_fm"] = nc.dram_tensor("p_fm", [8, 128, N], F32, kind="ExternalInput").ap()
    d["wo"] = nc.dram_tensor("wo", [8, 128, DM], F32, kind="ExternalInput").ap()
    d["bo_r"] = nc.dram_tensor("bo_r", [1, DM], F32, kind="ExternalInput").ap()
    d["x_out"] = nc.dram_tensor("x_out", [N, DM], F32, kind="ExternalOutput").ap()
    x_tm = b.sbuf("x_tm", [128, NT * DM], F32)
    pfm = b.sbuf("pfm", [128, NCH * N], BF16)
    wob = b.sbuf("wob", [128, 8 * DM], BF16)
    bor = b.sbuf("bor", [1, DM], BF16)
    ones_row = b.sbuf("ones_row", [1, 128], BF16)
    sy, g = ctx.sync, ctx.gpsimd
    b.s_ldx.inc(sy.dma_start(out=x_tm[:].rearrange("p (t d) -> p t d", t=NT),
                             in_=tm_view(d["x_in"])), 16)
    for c in range(8):
        b.s_ld.inc(g.dma_start(out=pfm[:, c * N:(c + 1) * N], in_=d["p_fm"][c]), 16)
        b.s_ld.inc(g.dma_start(out=wob[:, c * DM:(c + 1) * DM], in_=d["wo"][c]), 16)
    b.s_ld.inc(g.dma_start(out=bor[:], in_=d["bo_r"]), 16)
    b.s_v.inc(ctx.vector.memset(ones_row[:], 1.0))
    ctx.tensor.wait_ge(b.s_ld.h, b.s_ld.n)
    ctx.tensor.wait_ge(b.s_ldx.h, b.s_ldx.n)
    ctx.wait(ctx.tensor, b.s_v.mark())
    cm4, ps4 = b.psum_cm("ps4", [128, 4, 512], F32)
    pool4 = PsumPool(ctx, ps4, 4)
    fin = emit_proj_tm(b, x_tm, DM, pfm,
                       lambda c: wob[:, c * DM:(c + 1) * DM],
                       8, DM, pool4, ones_row, bias_row=bor, add_to=x_tm)
    ctx.wait(ctx.sync, fin)
    b.s_st.inc(ctx.sync.dma_start(out=tm_view(d["x_out"]),
                                  in_=x_tm[:].rearrange("p (t d) -> p t d", t=NT)), 16)
    ctx.sync.wait_ge(b.s_st.h, b.s_st.n)
    cm4.__exit__(None, None, None)
    ctx.install()
    return d


def prep_l0_attn_a(ins):
    out = {"wq": chunk_weight(np.asarray(ins["L0_attn_a_Wq"])),
           "wk": chunk_weight(np.asarray(ins["L0_attn_a_Wk"])),
           "wv": chunk_weight(np.asarray(ins["L0_attn_a_Wv"])),
           "wo": chunk_weight(np.asarray(ins["L0_attn_a_Wo"])),
           "bq_r": np.ascontiguousarray(np.asarray(ins["L0_attn_a_bq"])[None, :].astype(np.float32)),
           "bk_r": np.ascontiguousarray(np.asarray(ins["L0_attn_a_bk"])[None, :].astype(np.float32)),
           "bv_r": np.ascontiguousarray(np.asarray(ins["L0_attn_a_bv"])[None, :].astype(np.float32)),
           "bo_r": np.ascontiguousarray(np.asarray(ins["L0_attn_a_bo"])[None, :].astype(np.float32)),
           "ident": make_identity_bf16()}
    idx = np.pi / 2 * np.arange(1, N + 1, dtype=np.float64) / N
    s = np.sin(idx).astype(np.float32)
    c = np.cos(idx).astype(np.float32)
    out["sincol"] = np.ascontiguousarray(s.reshape(NT, 128).T)
    out["coscol"] = np.ascontiguousarray(c.reshape(NT, 128).T)
    return out


def run_l0_attn_a(x_full, ins, dbg=None):
    prep = prep_l0_attn_a(ins)
    with ExitStack() as stack:
        nc = bass.Bass("TRN2", target_bir_lowering=False)
        build_l0_attn_a(nc, stack, dbg=dbg)
        in_maps = []
        for i in range(8):
            m = dict(prep)
            m["x_in"] = np.ascontiguousarray(x_full[i])
            in_maps.append(m)
        res = run_bass_kernel_spmd(nc, in_maps, core_ids=list(range(8)))
    A = np.stack([res.results[i]["attn_fm"] for i in range(8)], axis=0)
    A = A.astype(np.float32)
    A2 = A.reshape(8, 128, 8, N).transpose(0, 3, 2, 1).reshape(8, N, DM)
    AP_ = A2.reshape(8 * N, DM)
    perm_src = np.arange(8 * N)
    lsrc = perm_src // 8
    bsrc = perm_src % 8
    P = AP_[bsrc * N + lsrc]
    P = P.reshape(8, N, DM)
    prepw = {"wo": prep["wo"], "bo_r": prep["bo_r"]}
    with ExitStack() as stack:
        nc = bass.Bass("TRN2", target_bir_lowering=False)
        build_l0_wo(nc, stack)
        in_maps = []
        for i in range(8):
            m = dict(prepw)
            m["x_in"] = np.ascontiguousarray(x_full[i])
            m["p_fm"] = np.ascontiguousarray(
                P[i].T.reshape(8, 128, N))
            in_maps.append(m)
        res2 = run_bass_kernel_spmd(nc, in_maps, core_ids=list(range(8)))
    out = np.stack([res2.results[i]["x_out"] for i in range(8)], axis=0)
    if dbg:
        return out, res.results
    return out


def build_l0_ffn(nc, stack, dbg=None, nlayers=6, nsubs=2):
    ctx = Ctx(nc, stack)
    b = B(ctx)
    d = {}
    DEPTH, HEADS, DH, W = 6, 16, 64, 256
    DI = 2730
    NOC = 22
    d["x_in"] = nc.dram_tensor("x_in", [N, DM], F32, kind="ExternalInput").ap()
    d["wqkv"] = nc.dram_tensor("wqkv", [DEPTH, 8, 128, 3072], F32, kind="ExternalInput").ap()
    d["wo"] = nc.dram_tensor("wo", [DEPTH, 8, 128, DM], F32, kind="ExternalInput").ap()
    d["w1"] = nc.dram_tensor("w1", [DEPTH, 8, 128, 5460], F32, kind="ExternalInput").ap()
    d["b1"] = nc.dram_tensor("b1", [DEPTH, 1, 5460], F32, kind="ExternalInput").ap()
    d["w2"] = nc.dram_tensor("w2", [DEPTH, NOC, 128, DM], F32, kind="ExternalInput").ap()
    d["b2"] = nc.dram_tensor("b2", [DEPTH, 1, DM], F32, kind="ExternalInput").ap()
    d["qsrow"] = nc.dram_tensor("qsrow", [DEPTH, 1, DH], F32, kind="ExternalInput").ap()
    d["ksrow"] = nc.dram_tensor("ksrow", [DEPTH, 1, DH], F32, kind="ExternalInput").ap()
    d["qsrow_sw"] = nc.dram_tensor("qsrow_sw", [DEPTH, 1, DH], F32, kind="ExternalInput").ap()
    d["ksrow_sw"] = nc.dram_tensor("ksrow_sw", [DEPTH, 1, DH], F32, kind="ExternalInput").ap()
    d["lng"] = nc.dram_tensor("lng", [DEPTH, 128, 8], F32, kind="ExternalInput").ap()
    d["lnb"] = nc.dram_tensor("lnb", [DEPTH, 128, 8], F32, kind="ExternalInput").ap()
    d["ident"] = nc.dram_tensor("ident", [128, 128], BF16, kind="ExternalInput").ap()
    d["onescol"] = nc.dram_tensor("onescol", [1, 128], F32, kind="ExternalInput").ap()
    for nm in ("csq", "ssq", "csk", "ssk"):
        d[nm] = nc.dram_tensor(nm, [128, NT * DH], BF16, kind="ExternalInput").ap()
    d["m0"] = nc.dram_tensor("m0", [128, 512], BF16, kind="ExternalInput").ap()
    d["m1"] = nc.dram_tensor("m1", [128, 512], BF16, kind="ExternalInput").ap()
    d["m0z"] = nc.dram_tensor("m0z", [128, 256], BF16, kind="ExternalInput").ap()
    d["m1z"] = nc.dram_tensor("m1z", [128, 256], BF16, kind="ExternalInput").ap()
    d["x_out"] = nc.dram_tensor("x_out", [N, DM], F32, kind="ExternalOutput").ap()

    xn = b.sbuf("xn", [128, NT * DM], F32)
    yT = b.sbuf("yT", [128, NCH * N], BF16)
    AR = 31744
    arena = b.sbuf("arena", [128, AR], BF16)
    attnout = arena[:, 0:16384]
    qfm = arena[:, 16384:18432]
    kfm = arena[:, 18432:20480]
    q_tm = arena[:, 20480:22528]
    k_tm = arena[:, 22528:24576]
    v_tm = arena[:, 24576:26624]
    eb = arena[:, 26624:27648]
    wob0 = arena[:, 27648:31744]
    hmid = arena[:, 0:22528]
    w2slc = arena[:, 22528:28160]
    yscr = b.sbuf("yscr", [128, 2, DM], BF16)
    stats = b.sbuf("stats", [128, 8 * NT], F32)
    sstat = b.sbuf("sstat", [128, 16 * NT], F32)
    sqscr = b.sbuf("sqscr", [128, DM], BF16)
    ident = b.sbuf("identt", [128, 128], BF16)
    ones_row = b.sbuf("ones_row", [1, 512], BF16)
    onescol = b.sbuf("onescol_t", [1, 128], F32)
    csq = b.sbuf("csq_t", [128, NT * DH], BF16)
    ssq = b.sbuf("ssq_t", [128, NT * DH], BF16)
    csk = b.sbuf("csk_t", [128, NT * DH], BF16)
    ssk = b.sbuf("ssk_t", [128, NT * DH], BF16)
    m0 = b.sbuf("m0_t", [128, 512], BF16)
    m1 = b.sbuf("m1_t", [128, 512], BF16)
    m0z = b.sbuf("m0z_t", [128, 256], BF16)
    m1z = b.sbuf("m1z_t", [128, 256], BF16)
    qsb = b.sbuf("qsb", [128, DH], F32)
    ksb = b.sbuf("ksb", [128, DH], F32)
    qsbw = b.sbuf("qsbw", [128, DH], F32)
    ksbw = b.sbuf("ksbw", [128, DH], F32)
    gcol = b.sbuf("gcol", [128, 8], F32)
    bcol = b.sbuf("bcol", [128, 8], F32)
    lscr = b.sbuf("lscr", [128, 2 * 128], F32)
    lsc2 = b.sbuf("lsc2", [128, 16], F32)
    s0 = b.sbuf("s0", [128, 2 * 512], F32)
    eT = b.sbuf("eT", [128, 2, 128], BF16)
    wvb = b.sbuf("wvb", [128, 2, 8 * 128], BF16)
    w1b = b.sbuf("w1b", [128, 2, 8 * 128], BF16)
    b1r = b.sbuf("b1r", [1, 5460], BF16)
    b2r = b.sbuf("b2r", [1, DM], BF16)
    zer = b.sbuf("zer", [1, 2], F32)
    xrl = s0[:, 0:DM]

    sy, g = ctx.sync, ctx.gpsimd
    t_q, v, a = ctx.tensor, ctx.vector, ctx.scalar

    b.s_ldx.inc(sy.dma_start(out=xn[:].rearrange("p (t d) -> p t d", t=NT),
                             in_=tm_view(d["x_in"])), 16)
    for nm, dst in (("ident", ident), ("csq", csq),
                    ("ssq", ssq), ("csk", csk), ("ssk", ssk), ("m0", m0),
                    ("m1", m1), ("m0z", m0z), ("m1z", m1z)):
        b.s_ld.inc(g.dma_start(out=dst[:], in_=d[nm]), 16)
    b.s_ldx.inc(sy.dma_start(out=onescol[:], in_=d["onescol"]), 16)
    x_mark = b.s_ldx.mark()
    const_mark = b.s_ld.mark()
    ctx.tensor.wait_ge(b.s_ld.h, const_mark[1])
    ctx.tensor.wait_ge(b.s_ldx.h, x_mark[1])
    b.s_v.inc(v.memset(ones_row[:], 1.0))
    b.s_v.inc(v.memset(zer[:], 0.0))
    for c in range(4, 8):
        b.s_v.inc(v.memset(yT[:, c * N:c * N + 1], 0.0))

    cm4, ps4 = b.psum_cm("ps4", [128, 4, 512], F32)
    pool4 = PsumPool(ctx, ps4, 4)
    cmT, psT = b.psum_cm("psT", [128, 2, 1024], BF16)
    psT_pool = PsumPool(ctx, psT, 2)
    cmAV, psAV = b.psum_cm("psAV", [128, 512], F32)

    apply_tile = emit_ln(b, xn, yscr, stats, sqscr, deps=[x_mark])
    for t in range(NT):
        apply_tile(t, xn[:, t * DM:(t + 1) * DM])
    ln0_mark = b.s_g.mark()

    def cast_shift_to_fm(src_tm, marks, fold=None, stats_cols=None):
        parity_busy = [None, None]
        for t in range(NT):
            p = t % 2
            if parity_busy[p] is not None:
                ctx.wait(g, parity_busy[p])
            for m in marks:
                ctx.wait(g, m)
            marks = ()
            if stats_cols is None:
                r = g.tensor_scalar(yscr[:, p, :], src_tm[:, t * DM:(t + 1) * DM],
                                    1.0, None, ALU.mult)
                b.s_g.inc(r)
            else:
                mu0, rstd0, mu1, rstd1 = stats_cols
                g.tensor_scalar(yscr[:, p, 0:512],
                                src_tm[:, t * DM:t * DM + 512],
                                mu0[:, t:t + 1], rstd0[:, t:t + 1],
                                ALU.subtract, ALU.mult)
                r = g.tensor_scalar(yscr[:, p, 512:1024],
                                    src_tm[:, t * DM + 512:(t + 1) * DM],
                                    mu1[:, t:t + 1], rstd1[:, t:t + 1],
                                    ALU.subtract, ALU.mult)
                b.s_g.inc(r)
            ctx.wait(t_q, b.s_g.mark())
            bank = psT_pool.acquire(t_q)
            for c in range(NCH):
                mm = t_q.transpose(psT[:, bank, c * 128:(c + 1) * 128],
                                   yscr[:, p, c * 128:(c + 1) * 128], ident[:])
            b.s_t.inc(mm)
            parity_busy[p] = b.s_t.mark()
            ctx.wait(v, b.s_t.mark())
            for c in range(NCH):
                base = c * N
                if c >= 4:
                    lo = base + t * 128 + 1
                    wd = min(128, base + N - lo)
                else:
                    lo = base + t * 128
                    wd = 128
                srcap = psT[:, bank, c * 128:c * 128 + wd]
                if fold is not None:
                    e = v.scalar_tensor_tensor(yT[:, lo:lo + wd], srcap,
                                               fold[0][:, c:c + 1],
                                               fold[1][:, c:c + 1].broadcast_to((128, wd)),
                                               ALU.mult, ALU.add)
                else:
                    e = v.tensor_copy(yT[:, lo:lo + wd], srcap)
            b.s_v.inc(e)
            psT_pool.release(bank, b.s_v.mark())
        return b.s_v.mark()

    busy = {}

    def load_slice(buf2, key, dram_ap, rr_state):
        par = rr_state[0] % 2
        rr_state[0] += 1
        bk = (key, par)
        if busy.get(bk) is not None:
            ctx.wait(g, busy[bk])
        nchk = dram_ap.shape[0]
        ow = dram_ap.shape[2]
        b.s_ld.inc(g.dma_start(
            out=buf2[:, par, 0:nchk * ow].rearrange("p (c o) -> p c o", c=nchk),
            in_=dram_ap.rearrange("c p o -> p c o")), 16)
        return par, b.s_ld.mark()

    def load_flat(buf_ap, dram_ap, key=None):
        if key is not None and busy.get(key) is not None:
            ctx.wait(g, busy[key])
        nchk = dram_ap.shape[0]
        ow = dram_ap.shape[2]
        b.s_ld.inc(g.dma_start(
            out=buf_ap[:, 0:nchk * ow].rearrange("p (c o) -> p c o", c=nchk),
            in_=dram_ap.rearrange("c p o -> p c o")), 16)
        return b.s_ld.mark()

    def emit_colshift(src_col, dst_col, direction):
        if direction == -1:
            b.s_ldx.inc(sy.dma_start(out=dst_col[1:128, :], in_=src_col[0:127, :]), 16)
            b.s_ldx.inc(sy.dma_start(out=dst_col[0:1, 1:NT], in_=src_col[127:128, 0:NT - 1]), 16)
        else:
            b.s_ldx.inc(sy.dma_start(out=dst_col[0:127, :], in_=src_col[1:128, :]), 16)
            b.s_ldx.inc(sy.dma_start(out=dst_col[127:128, 0:NT - 1], in_=src_col[0:1, 1:NT]), 16)

    rr_qkv = [0]
    rr_w1 = [0]
    prev_ffn_mark = ln0_mark
    attn_gates = {"sub": None, "eb_free": None, "s0_free": None}
    for li in range(nlayers):
        sh_mark = cast_shift_to_fm(xn, [prev_ffn_mark])
        tbl_gate = b.s_g.mark() if li > 0 else None
        tbl_marks = []
        for nm, dst in (("csq", csq), ("ssq", ssq), ("csk", csk), ("ssk", ssk)):
            if li > 0:
                if tbl_gate is not None:
                    ctx.wait(g, tbl_gate)
                    tbl_gate = None
                b.s_ld.inc(g.dma_start(out=dst[:], in_=d[nm]), 16)
        tbl_ld = b.s_ld.mark()
        for buf, dr in ((qsb, "qsrow"), (ksb, "ksrow"),
                        (qsbw, "qsrow_sw"), (ksbw, "ksrow_sw")):
            b.s_ld.inc(g.dma_start(out=buf[0:1, :], in_=d[dr][li]), 16)
        qs_ld = b.s_ld.mark()
        ctx.wait(t_q, qs_ld)
        brd = []
        for buf in (qsb, ksb, qsbw, ksbw):
            bank = pool4.acquire(t_q)
            b.s_t.inc(t_q.matmul(ps4[:, bank, 0:DH], onescol[:], buf[0:1, :],
                                 start=True, stop=True))
            brd.append((buf, bank, b.s_t.mark()))
        for buf, bank, mk in brd:
            ctx.wait(a, mk)
            b.s_a.inc(a.activation(buf[:], ps4[:, bank, 0:DH], AF.Copy,
                                   bias=0.0, scale=1.0))
            pool4.release(bank, b.s_a.mark())
        qsb_mark = b.s_a.mark()
        ctx.wait(g, qsb_mark)
        ctx.wait(g, tbl_ld)
        ctx.wait(v, qsb_mark)
        ctx.wait(v, tbl_ld)
        for tb, sc, q_ in ((csq, qsb, g), (ssq, qsbw, v), (csk, ksb, g), (ssk, ksbw, v)):
            t3v = tb[:].rearrange("p (t e) -> p t e", e=DH)
            s3v = sc[:].rearrange("p (o e) -> p o e", o=1).broadcast_to((128, NT, DH))
            r = q_.tensor_tensor(t3v, t3v, s3v, ALU.mult)
            (b.s_g if q_ is g else b.s_v).inc(r)
        tblm = (b.s_g.mark(), b.s_v.mark())

        for hp in range(8):
            for wi, dst in ((0, q_tm), (1, k_tm), (2, v_tm)):
                par, wm = load_slice(
                    wvb, "qkv",
                    d["wqkv"][li, :, :, wi * 1024 + hp * 128:wi * 1024 + (hp + 1) * 128],
                    rr_qkv)
                emit_proj_tm(b, dst, 128, yT,
                             lambda c, par=par: wvb[:, par, c * 128:(c + 1) * 128],
                             NCH, 128, pool4, ones_row,
                             ready=[sh_mark, wm, qsb_mark])
                busy[("qkv", par)] = b.s_t.mark()
            pm = b.s_v.mark()
            ctx.wait(g, pm)
            ctx.wait(a, pm)
            ctx.wait(g, tblm[1])
            ctx.wait(a, tblm[0])
            ctx.wait(a, tblm[1])
            ctx.wait(v, tblm[0])
            for (buf, ct, st) in ((q_tm, csq, ssq), (k_tm, csk, ssk)):
                for t in range(NT):
                    sl3 = buf[:, t * 128:(t + 1) * 128] \
                        .rearrange("p (h e) -> p h e", h=2)
                    b.s_a.inc(a.activation(lscr[:, 0:128],
                                           buf[:, t * 128:(t + 1) * 128], AF.Square))
                    ctx.wait(v, b.s_a.mark())
                    b.s_v.inc(v.reduce_sum(lsc2[:, 0:2],
                                           lscr[:, 0:128].rearrange("p (h e) -> p h e", h=2),
                                           axis=AX.X))
                    ctx.wait(a, b.s_v.mark())
                    b.s_a.inc(a.activation(lsc2[:, 2:4], lsc2[:, 0:2], AF.Sqrt,
                                           bias=0.0, scale=1.0))
                    ctx.wait(g, b.s_a.mark())
                    b.s_g.inc(g.tensor_scalar(lsc2[:, 4:6], lsc2[:, 2:4], 1e-12,
                                              None, ALU.max))
                    ctx.wait(v, b.s_g.mark())
                    b.s_v.inc(v.reciprocal(lsc2[:, 6:8], lsc2[:, 4:6]))
                    ctx.wait(g, b.s_v.mark())
                    rn3 = lsc2[:, 6:8].rearrange("p (h o) -> p h o", o=1) \
                        .broadcast_to((128, 2, DH))
                    q23 = lscr[:, 128:256].rearrange("p (h e) -> p h e", h=2)
                    b.s_g.inc(g.tensor_tensor(q23, sl3, rn3, ALU.mult))
                    q2m = b.s_g.mark()
                    ctx.wait(v, q2m)
                    ct3 = ct[:, t * DH:(t + 1) * DH] \
                        .rearrange("p (o e) -> p o e", o=1).broadcast_to((128, 2, DH))
                    st3 = st[:, t * DH:(t + 1) * DH] \
                        .rearrange("p (o e) -> p o e", o=1)
                    pA = sqscr[:, 0:128].rearrange("p (h e) -> p h e", h=2)
                    b.s_v.inc(v.tensor_tensor(pA, q23, ct3, ALU.mult))
                    ctx.wait(a, b.s_v.mark())
                    b.s_a.inc(a.activation(sqscr[:, 128:256], sqscr[:, 0:128],
                                           AF.Copy, bias=0.0, scale=1.0))
                    ctx.wait(g, b.s_a.mark())
                    pB = sqscr[:, 256:384].rearrange("p (h e) -> p h e", h=2)
                    for half in range(2):
                        r = g.tensor_tensor(
                            pB[:, :, half * 32:half * 32 + 32],
                            q23[:, :, (1 - half) * 32:(1 - half) * 32 + 32],
                            st3[:, :, half * 32:half * 32 + 32]
                            .broadcast_to((128, 2, 32)), ALU.mult)
                        b.s_g.inc(r)
                    ctx.wait(v, b.s_g.mark())
                    b.s_v.inc(v.tensor_tensor(
                        sl3, sqscr[:, 128:256].rearrange("p (h e) -> p h e", h=2),
                        pB, ALU.add))
            qk_mark = b.s_v.mark()
            ctx.wait(g, qk_mark)
            ctx.wait(a, qk_mark)
            b.s_g.n += 0
            for (srcb, dstf) in ((q_tm, qfm), (k_tm, kfm)):
                ctx.wait(t_q, qk_mark)
                for t in range(NT):
                    bank = psT_pool.acquire(t_q)
                    tr = t_q.transpose(psT[:, bank, 0:128],
                                       srcb[:, t * 128:(t + 1) * 128], ident[:])
                    b.s_t.inc(tr)
                    ctx.wait(a, b.s_t.mark())
                    ev = a.activation(dstf[:, t * 128:(t + 1) * 128],
                                      psT[:, bank, 0:128], AF.Copy, bias=0.0, scale=1.0)
                    b.s_a.inc(ev)
                    psT_pool.release(bank, b.s_a.mark())
            tfm_mark = b.s_a.mark()
            ctx.wait(t_q, tfm_mark)
            for w in range(8):
                kw = 512 if w > 0 else 256
                k0 = (w - 1) * 256 if w > 0 else 0
                nkt = kw // 128
                for q2i in range(2):
                    t = w * 2 + q2i
                    mask = (m1 if q2i else m0) if w > 0 else (m1z if q2i else m0z)
                    sbanks = []
                    for hi in range(2):
                        h0 = hi * 64
                        bank = pool4.acquire(t_q)
                        t_q.matmul(ps4[:, bank, 0:kw],
                                   qfm[h0:h0 + 64, t * 128:(t + 1) * 128],
                                   kfm[h0:h0 + 64, k0:k0 + kw],
                                   start=True, stop=False)
                        mm = t_q.matmul(ps4[:, bank, 0:kw], ident[:],
                                        mask[:, 0:kw], start=False, stop=True)
                        b.s_t.inc(mm)
                        sbanks.append((hi, bank, b.s_t.mark()))
                    if attn_gates["sub"] is not None:
                        ctx.wait(a, attn_gates["sub"])
                    if attn_gates.get("norm") is not None:
                        ctx.wait(a, attn_gates["norm"])
                    for hi, bank, mk in sbanks:
                        ctx.wait(a, mk)
                        ev = a.activation(s0[:, hi * 512:hi * 512 + kw],
                                          ps4[:, bank, 0:kw], AF.Copy,
                                          bias=0.0, scale=8.0)
                        b.s_a.inc(ev)
                        pool4.release(bank, b.s_a.mark())
                    ctx.wait(v, b.s_a.mark())
                    s03 = s0[:].rearrange("p (h e) -> p h e", h=2)[:, :, 0:kw]
                    b.s_v.inc(v.reduce_max(lsc2[:, 8:10], s03, axis=AX.X))
                    ctx.wait(g, b.s_v.mark())
                    mx3 = lsc2[:, 8:10].rearrange("p (h o) -> p h o", o=1) \
                        .broadcast_to((128, 2, kw))
                    b.s_g.inc(g.tensor_tensor(s03, s03, mx3, ALU.subtract))
                    attn_gates["sub"] = b.s_g.mark()
                    ctx.wait(a, attn_gates["sub"])
                    if attn_gates["eb_free"] is not None:
                        ctx.wait(a, attn_gates["eb_free"])
                    for hi in range(2):
                        ev = a.activation(eb[:, hi * 512:hi * 512 + kw],
                                          s0[:, hi * 512:hi * 512 + kw], AF.Exp,
                                          accum_out=lsc2[:, 10 + hi:11 + hi])
                        b.s_a.inc(ev)
                    ctx.wait(v, b.s_a.mark())
                    b.s_v.inc(v.reciprocal(lsc2[:, 12:14], lsc2[:, 10:12]))
                    ctx.wait(g, b.s_v.mark())
                    rs3 = lsc2[:, 12:14].rearrange("p (h o) -> p h o", o=1) \
                        .broadcast_to((128, 2, kw))
                    eb3 = eb[:].rearrange("p (h e) -> p h e", h=2)[:, :, 0:kw]
                    b.s_g.inc(g.tensor_tensor(eb3, eb3, rs3, ALU.mult))
                    en_mark = b.s_g.mark()
                    attn_gates["norm"] = en_mark
                    ctx.wait(t_q, en_mark)
                    for hi in range(2):
                        h0 = hi * 64
                        av_marks = {}
                        for kt in range(nkt):
                            bankT = psT_pool.acquire(t_q)
                            tr = t_q.transpose(psT[:, bankT, 0:128],
                                               eb[:, hi * 512 + kt * 128:
                                                  hi * 512 + (kt + 1) * 128], ident[:])
                            b.s_t.inc(tr)
                            ctx.wait(v, b.s_t.mark())
                            if kt >= 2:
                                ctx.wait(v, av_marks[kt - 2])
                            b.s_v.inc(v.tensor_copy(eT[:, kt % 2, :],
                                                    psT[:, bankT, 0:128]))
                            evm = b.s_v.mark()
                            psT_pool.release(bankT, evm)
                            ctx.wait(t_q, evm)
                            ka = (k0 // 128) + kt
                            mm = t_q.matmul(psAV[0:64, 0:128],
                                            v_tm[:, ka * 128 + h0:ka * 128 + h0 + 64],
                                            eT[:, kt % 2, :],
                                            start=(kt == 0), stop=(kt == nkt - 1))
                            b.s_t.inc(mm)
                            av_marks[kt] = b.s_t.mark()
                        ctx.wait(v, b.s_t.mark())
                        dsta = attnout[h0:h0 + 64,
                                       hp * N + t * 128:hp * N + (t + 1) * 128]
                        b.s_v.inc(v.tensor_copy(dsta, psAV[0:64, 0:128]))
                        ctx.wait(t_q, b.s_v.mark())
                    attn_gates["eb_free"] = b.s_t.mark()
        for oc2 in range(2):
            wm = load_flat(wob0, d["wo"][li, :, :, oc2 * 512:(oc2 + 1) * 512],
                           key="wo")
            fin = emit_proj_tm(b, xn, DM, attnout,
                               lambda c: wob0[:, c * 512:(c + 1) * 512],
                               NCH, 512, pool4, ones_row,
                               ready=[b.s_v.mark(), wm], add_to=xn,
                               dst_off=oc2 * 512)
            busy["wo"] = b.s_t.mark()
        attn_res_mark = b.s_v.mark()
        prev_ffn_mark = attn_res_mark
        if nsubs == 1 and li == nlayers - 1:
            break

        Sa = sstat[:, 0:NT]
        Ss = sstat[:, NT:2 * NT]
        Qa = sstat[:, 2 * NT:3 * NT]
        Qs = sstat[:, 3 * NT:4 * NT]
        Ssm1 = sstat[:, 4 * NT:5 * NT]
        Sap1 = sstat[:, 5 * NT:6 * NT]
        Qsm1 = sstat[:, 6 * NT:7 * NT]
        Qap1 = sstat[:, 7 * NT:8 * NT]
        mu0 = sstat[:, 8 * NT:9 * NT]
        rstd0 = sstat[:, 9 * NT:10 * NT]
        mu1 = sstat[:, 10 * NT:11 * NT]
        rstd1 = sstat[:, 11 * NT:12 * NT]
        t0_ = sstat[:, 12 * NT:13 * NT]
        t1_ = sstat[:, 13 * NT:14 * NT]
        t2_ = sstat[:, 14 * NT:15 * NT]
        t3_ = sstat[:, 15 * NT:16 * NT]
        ctx.wait(v, attn_res_mark)
        ctx.wait(a, attn_res_mark)
        for t in range(NT):
            v.reduce_sum(Sa[:, t:t + 1], xn[:, t * DM:t * DM + 512], axis=AX.X)
            v.reduce_sum(Ss[:, t:t + 1], xn[:, t * DM + 512:(t + 1) * DM], axis=AX.X)
            a.activation(sqscr[:, 0:512], xn[:, t * DM:t * DM + 512],
                         AF.Square, accum_out=Qa[:, t:t + 1])
            a.activation(sqscr[:, 512:1024], xn[:, t * DM + 512:(t + 1) * DM],
                         AF.Square, accum_out=Qs[:, t:t + 1])
        b.s_v.inc(v.recs[-1])
        b.s_a.inc(a.recs[-1])
        hs_mark = (b.s_v.mark(), b.s_a.mark())
        ctx.wait(sy, hs_mark[0])
        ctx.wait(sy, hs_mark[1])
        for bnd in (Ssm1[0:1, 0:1], Sap1[127:128, NT - 1:NT],
                    Qsm1[0:1, 0:1], Qap1[127:128, NT - 1:NT]):
            b.s_ldx.inc(sy.dma_start(out=bnd, in_=zer[:, 0:1]), 16)
        emit_colshift(Ss, Ssm1, -1)
        emit_colshift(Sa, Sap1, +1)
        emit_colshift(Qs, Qsm1, -1)
        emit_colshift(Qa, Qap1, +1)
        sh_ld = b.s_ldx.mark()
        ctx.wait(g, sh_ld)
        g.tensor_tensor(t0_, Sa, Ssm1, ALU.add)
        g.tensor_tensor(t1_, Sap1, Ss, ALU.add)
        g.tensor_tensor(t2_, Qa, Qsm1, ALU.add)
        b.s_g.inc(g.tensor_tensor(t3_, Qap1, Qs, ALU.add))
        ctx.wait(a, b.s_g.mark())
        a.mul(mu0, t0_, 1.0 / DM)
        b.s_a.inc(a.mul(mu1, t1_, 1.0 / DM))
        ctx.wait(v, b.s_a.mark())
        v.tensor_scalar(t0_, t2_, 1.0 / DM, 1e-5, ALU.mult, ALU.add)
        v.tensor_scalar(t1_, t3_, 1.0 / DM, 1e-5, ALU.mult, ALU.add)
        v.tensor_tensor(t2_, mu0, mu0, ALU.mult)
        b.s_v.inc(v.tensor_tensor(t3_, mu1, mu1, ALU.mult))
        ctx.wait(g, b.s_v.mark())
        g.tensor_tensor(t0_, t0_, t2_, ALU.subtract)
        b.s_g.inc(g.tensor_tensor(t1_, t1_, t3_, ALU.subtract))
        ctx.wait(a, b.s_g.mark())
        a.activation(t2_, t0_, AF.Sqrt, bias=0.0, scale=1.0)
        b.s_a.inc(a.activation(t3_, t1_, AF.Sqrt, bias=0.0, scale=1.0))
        ctx.wait(v, b.s_a.mark())
        v.reciprocal(rstd0, t2_)
        b.s_v.inc(v.reciprocal(rstd1, t3_))
        st_mark = b.s_v.mark()
        ctx.wait(g, st_mark)
        b.s_ldx.inc(sy.dma_start(out=gcol[:], in_=d["lng"][li]), 16)
        b.s_ldx.inc(sy.dma_start(out=bcol[:], in_=d["lnb"][li]), 16)
        gb_mark = b.s_ldx.mark()
        ctx.wait(v, gb_mark)
        ffn_in_mark = cast_shift_to_fm(xn, [st_mark], fold=(gcol, bcol),
                                       stats_cols=(mu0, rstd0, mu1, rstd1))
        b.s_ld.inc(g.dma_start(out=b1r[:], in_=d["b1"][li]), 16)
        b.s_ld.inc(g.dma_start(out=b2r[:], in_=d["b2"][li]), 16)
        br_mark = b.s_ld.mark()
        for half in range(2):
            hbase = half * 1024
            first_oc_gate = [attn_gates["eb_free"], b.s_v.mark()]
            for oc in range(NOC):
                ow = min(128, DI - oc * 128)
                par, wm = load_slice(w1b, "w1a",
                                     d["w1"][li, :, :, oc * 128:oc * 128 + ow],
                                     rr_w1)
                par2, wm2 = load_slice(w1b, "w1g",
                                       d["w1"][li, :, :, DI + oc * 128:DI + oc * 128 + ow],
                                       rr_w1)
                ctx.wait(t_q, wm)
                ctx.wait(t_q, wm2)
                if oc == 0:
                    ctx.wait(t_q, br_mark)
                    ctx.wait(t_q, ffn_in_mark)
                for tg in range(2):
                    banka = pool4.acquire(t_q)
                    for c in range(NCH):
                        t_q.matmul(ps4[0:ow, banka, :],
                                   w1b[:, par, c * ow:(c + 1) * ow],
                                   yT[:, c * N + hbase + tg * 512:
                                      c * N + hbase + (tg + 1) * 512],
                                   start=(c == 0), stop=False)
                    mm = t_q.matmul(ps4[0:ow, banka, :],
                                    b1r[:, oc * 128:oc * 128 + ow],
                                    ones_row[:], start=False, stop=True)
                    b.s_t.inc(mm)
                    bankg = pool4.acquire(t_q)
                    for c in range(NCH):
                        t_q.matmul(ps4[0:ow, bankg, :],
                                   w1b[:, par2, c * ow:(c + 1) * ow],
                                   yT[:, c * N + hbase + tg * 512:
                                      c * N + hbase + (tg + 1) * 512],
                                   start=(c == 0), stop=False)
                    mm = t_q.matmul(ps4[0:ow, bankg, :],
                                    b1r[:, DI + oc * 128:DI + oc * 128 + ow],
                                    ones_row[:], start=False, stop=True)
                    b.s_t.inc(mm)
                    gm = b.s_t.mark()
                    ctx.wait(a, gm)
                    for gate_m in first_oc_gate:
                        if gate_m is not None:
                            ctx.wait(a, gate_m)
                    first_oc_gate = []
                    ev = a.activation(sqscr[0:ow, 0:512], ps4[0:ow, bankg, :],
                                      AF.Gelu)
                    b.s_a.inc(ev)
                    pool4.release(bankg, b.s_a.mark())
                    ctx.wait(v, b.s_a.mark())
                    tt = v.tensor_tensor(hmid[0:ow, oc * 1024 + tg * 512:
                                              oc * 1024 + (tg + 1) * 512],
                                         ps4[0:ow, banka, :], sqscr[0:ow, 0:512],
                                         ALU.mult)
                    b.s_v.inc(tt)
                    pool4.release(banka, b.s_v.mark())
                busy[("w1a", par)] = b.s_t.mark()
                busy[("w1g", par2)] = b.s_t.mark()
            hm_mark = b.s_v.mark()
            for og in range(4):
                wm = load_flat(w2slc, d["w2"][li, :, :, og * 256:(og + 1) * 256],
                               key="w2")
                ctx.wait(t_q, wm)
                ctx.wait(t_q, hm_mark)
                for tl in range(8):
                    T = half * 8 + tl
                    bank = pool4.acquire(t_q)
                    for c in range(NOC):
                        cw = min(128, DI - c * 128)
                        t_q.matmul(ps4[:, bank, 0:256],
                                   hmid[0:cw, c * 1024 + tl * 128:c * 1024 + (tl + 1) * 128],
                                   w2slc[0:cw, c * 256:(c + 1) * 256],
                                   start=(c == 0), stop=False)
                    mm = t_q.matmul(ps4[:, bank, 0:256], ones_row[:, 0:128],
                                    b2r[:, og * 256:(og + 1) * 256],
                                    start=False, stop=True)
                    b.s_t.inc(mm)
                    ctx.wait(v, b.s_t.mark())
                    dst = xn[:, T * DM + og * 256:T * DM + (og + 1) * 256]
                    ev = v.tensor_add(dst, ps4[:, bank, 0:256], dst)
                    b.s_v.inc(ev)
                    pool4.release(bank, b.s_v.mark())
                busy["w2"] = b.s_t.mark()
        prev_ffn_mark = b.s_v.mark()

    ctx.wait(sy, prev_ffn_mark)
    for t in range(NT):
        b.s_ldx.inc(sy.dma_start(
            out=xrl[:],
            in_=d["x_in"].rearrange("(t p) d -> p t d", p=128)[:, t, :]), 16)
        ctx.wait(g, b.s_ldx.mark())
        r = g.tensor_tensor(xn[:, t * DM:(t + 1) * DM],
                            xn[:, t * DM:(t + 1) * DM], xrl[:], ALU.add)
        b.s_g.inc(r)
        ctx.wait(sy, b.s_g.mark())
    b.s_st.inc(sy.dma_start(out=tm_view(d["x_out"]),
                            in_=xn[:].rearrange("p (t d) -> p t d", t=NT)), 16)
    if dbg:
        dbg_bufs = {"yT": yT, "attnout": attnout, "q_tm": q_tm, "k_tm": k_tm,
                    "v_tm": v_tm, "qfm": qfm, "kfm": kfm, "hmid": hmid,
                    "xn": xn, "sstat": sstat, "qsb": qsb, "csq": csq, "ssq": ssq}
        for nm in dbg:
            buf = dbg_bufs[nm]
            if not isinstance(buf, bass.AP):
                buf = buf[:]
            dt_ = F32 if nm in ("xn", "sstat", "qsb") else BF16
            dd_ = nc.dram_tensor("dbg_" + nm, list(buf.shape), dt_,
                                 kind="ExternalOutput").ap()
            b.s_st.inc(ctx.sync.dma_start(out=dd_, in_=buf), 16)
    ctx.sync.wait_ge(b.s_st.h, b.s_st.n)
    cmAV.__exit__(None, None, None)
    cmT.__exit__(None, None, None)
    cm4.__exit__(None, None, None)
    ctx.install()
    return d


def prep_l0_ffn(ins):
    DEPTH, DH, W = 6, 64, 256
    DI = 2730
    wqkv = np.asarray(ins["L0_ffn_Wqkv"])
    wo = np.asarray(ins["L0_ffn_Wo"])
    w1 = np.asarray(ins["L0_ffn_W1"])
    b1 = np.asarray(ins["L0_ffn_b1"])
    w2 = np.asarray(ins["L0_ffn_W2"])
    b2 = np.asarray(ins["L0_ffn_b2"])
    qs = np.asarray(ins["L0_ffn_q_scale"])
    ks = np.asarray(ins["L0_ffn_k_scale"])
    lng = np.asarray(ins["L0_ffn_ln_g"])
    lnb = np.asarray(ins["L0_ffn_ln_b"])
    out = {}
    out["wqkv"] = np.stack([chunk_weight(wqkv[i]) for i in range(DEPTH)])
    out["wo"] = np.stack([chunk_weight(wo[i]) for i in range(DEPTH)])
    out["w1"] = np.stack([chunk_weight(w1[i]) for i in range(DEPTH)])
    out["w2"] = np.stack([chunk_weight(w2[i]) for i in range(DEPTH)])
    out["b1"] = np.ascontiguousarray(b1[:, None, :].astype(np.float32))
    out["b2"] = np.ascontiguousarray(b2[:, None, :].astype(np.float32))
    out["qsrow"] = np.ascontiguousarray(qs[:, None, :].astype(np.float32))
    out["ksrow"] = np.ascontiguousarray(ks[:, None, :].astype(np.float32))
    perm = np.concatenate([np.arange(32, 64), np.arange(0, 32)])
    out["qsrow_sw"] = np.ascontiguousarray(qs[:, None, perm].astype(np.float32))
    out["ksrow_sw"] = np.ascontiguousarray(ks[:, None, perm].astype(np.float32))
    out["lng"] = np.ascontiguousarray(
        lng.reshape(DEPTH, 8, 128).transpose(0, 2, 1).astype(np.float32))
    out["lnb"] = np.ascontiguousarray(
        lnb.reshape(DEPTH, 8, 128).transpose(0, 2, 1).astype(np.float32))
    out["ident"] = make_identity_bf16()
    out["onescol"] = np.ones((1, 128), np.float32)
    half = np.arange(0, DH, 2, dtype=np.float64)
    inv_freq = 1.0 / (10000.0 ** (half / DH))
    t = np.arange(N, dtype=np.float64)
    freqs = np.concatenate([t[:, None] * inv_freq[None, :]] * 2, axis=1)
    base_scale = (half + 0.4 * DH) / (1.4 * DH)
    power = (t - N // 2) / (W // 2)
    scale = np.concatenate([base_scale[None, :] ** power[:, None]] * 2, axis=1)
    cos = np.cos(freqs)
    sin = np.sin(freqs)
    sgn = np.ones(DH)
    sgn[:32] = -1.0
    csq_t = (cos * scale).astype(np.float32)
    ssq_t = (sin * scale * sgn).astype(np.float32)
    csk_t = (cos / scale).astype(np.float32)
    ssk_t = (sin / scale * sgn).astype(np.float32)
    def tmify(tab):
        o = np.zeros((128, NT * DH), np.float32)
        for tt in range(NT):
            o[:, tt * DH:(tt + 1) * DH] = tab[tt * 128:(tt + 1) * 128]
        return o
    out["csq"] = tmify(csq_t).astype(BF)
    out["ssq"] = tmify(ssq_t).astype(BF)
    out["csk"] = tmify(csk_t).astype(BF)
    out["ssk"] = tmify(ssk_t).astype(BF)
    BIG = np.float32(-1e30)
    def mk_mask(q2i, w0):
        ncols = 256 if w0 else 512
        i = np.arange(128)[:, None] + q2i * 128
        j = np.arange(ncols)[None, :]
        qa = i + (0 if w0 else 256)
        m = (qa < j) | (qa > j + W)
        return np.where(m, BIG, np.float32(0)).astype(BF)
    out["m0"] = mk_mask(0, False)
    out["m1"] = mk_mask(1, False)
    out["m0z"] = mk_mask(0, True)
    out["m1z"] = mk_mask(1, True)
    return out


def run_l0_ffn(x_full, ins, dbg=None, nlayers=6, nsubs=2):
    prep = prep_l0_ffn(ins)
    with ExitStack() as stack:
        nc = bass.Bass("TRN2", target_bir_lowering=False)
        build_l0_ffn(nc, stack, dbg=dbg, nlayers=nlayers, nsubs=nsubs)
        in_maps = []
        for i in range(8):
            m = dict(prep)
            m["x_in"] = np.ascontiguousarray(x_full[i])
            in_maps.append(m)
        res = run_bass_kernel_spmd(nc, in_maps, core_ids=list(range(8)))
    out = np.stack([res.results[i]["x_out"] for i in range(8)], axis=0)
    if dbg:
        return out, res.results
    return out


def kernel(**inputs):
    x = np.ascontiguousarray(np.asarray(inputs["L0_attn_a_query"], dtype=np.float32))
    x = run_l0_attn_a(x, inputs)
    x = run_l0_ffn(x, inputs)
    x = run_l0_attn_b(x, inputs)
    x = run_l0_ffn(x, inputs)
    x = run_l1_attn_a(x, inputs)
    x = run_l1_ffn(x, inputs)
    x = run_l1_attn_b(x, inputs)
    x = run_l1_ffn(x, inputs)
    return x
